# version 10
# baseline (speedup 1.0000x reference)
import numpy as np
import ml_dtypes
from contextlib import ExitStack

import concourse.bass as bass
import concourse.mybir as mybir
from concourse.bass_utils import run_bass_kernel_spmd

F32 = mybir.dt.float32
BF16 = mybir.dt.bfloat16
AF = mybir.ActivationFunctionType
ALU = mybir.AluOpType
AX = mybir.AxisListType

T = 2048
D = 1024
NT = T // 128
HEADS = 8
DH = 128
NLOC = 4
NW = 16
NB = 32
DEPTH = 4
DFF = 4096
NE = 64
FE = 512

SUMS, NEGMU, VARSS, STD, RECR, MUR = 0, 16, 32, 48, 64, 80
WS0 = 96
QS0 = 108
LRC = 120
GT1, BT1, GT2, BT2 = 128, 136, 144, 152
B1T = 160
EB = 192
CW = 256
TMP = CW + NT * 64


class Eng:
    def __init__(self, bass_eng, sem, name):
        self.e = bass_eng
        self.sem = sem
        self.name = name
        self.count = 0
        self.waited = {}
        self.last_inst = None
        self.last_tok = None

    def mark(self):
        if self.last_tok is not None:
            return self.last_tok
        if self.last_inst is None:
            return None
        try:
            self.last_inst.then_inc(self.sem, 1)
        except Exception:
            self.e.maybe_drain_then_inc((self.sem, 1))
        self.count += 1
        self.last_tok = (self.sem, self.count, self)
        return self.last_tok

    def emit(self, inst):
        self.last_inst = inst
        self.last_tok = None
        return inst


class Prog:
    def __init__(self, nc, stack):
        self.nc = nc
        e = stack.enter_context
        self.te = Eng(nc.tensor, e(nc.semaphore("s_te")), "te")
        self.act = Eng(nc.scalar, e(nc.semaphore("s_act")), "act")
        self.dve = Eng(nc.vector, e(nc.semaphore("s_dve")), "dve")
        self.sp = Eng(nc.sync, e(nc.semaphore("s_sp")), "sp")
        self.gp = Eng(nc.gpsimd, e(nc.semaphore("s_gp")), "gp")
        self.pools = {}
        for kind, n in (("w", 10), ("x", 8), ("o", 4)):
            self.pools[kind] = {
                "sems": [e(nc.semaphore("s_%sdma%d" % (kind, j)))
                         for j in range(n)],
                "uses": [0] * n,
                "i": 0,
            }

    def wait(self, eng, tok):
        if tok is None:
            return
        sem, val, owner = tok
        key = id(sem)
        if eng.waited.get(key, -1) >= val:
            return
        eng.e.wait_ge(sem, val)
        eng.waited[key] = val

    def waits(self, eng, toks):
        for t in toks:
            self.wait(eng, t)

    def selfbar(self, eng):
        self.wait(eng, eng.mark())

    def dma(self, queue_eng, kind, out_ap, in_ap, deps=()):
        self.waits(queue_eng, deps)
        pool = self.pools[kind]
        idx = pool["i"] % len(pool["sems"])
        pool["i"] += 1
        sem = pool["sems"][idx]
        prior = pool["uses"][idx]
        if prior:
            self.wait(queue_eng, (sem, 16 * prior, None))
        inst = queue_eng.e.dma_start(out_ap, in_ap)
        inst.then_inc(sem, 16)
        pool["uses"][idx] = prior + 1
        queue_eng.emit(inst)
        return (sem, 16 * (prior + 1), None)


class Guard:

    def __init__(self, prog):
        self.p = prog
        self.marks = {}

    def use(self, eng):
        for name, tok in list(self.marks.items()):
            self.p.wait(eng, tok)

    def done(self, eng):
        self.marks[eng.name] = eng.mark()

    def done_tok(self, name, tok):
        self.marks[name] = tok


def build_program(n_layers=DEPTH, do_top=True, do_moe=True, n_experts=NE,
                  taps=None, stop_after=None):
    taps = taps or []
    nc = bass.Bass("TRN2", target_bir_lowering=False)

    xin = nc.dram_tensor("x", [T, D], F32, kind="ExternalInput").ap()
    Wq = nc.dram_tensor("Wq", [DEPTH, D, D], F32, kind="ExternalInput").ap()
    Wk = nc.dram_tensor("Wk", [DEPTH, D, D], F32, kind="ExternalInput").ap()
    Wv = nc.dram_tensor("Wv", [DEPTH, D, D], F32, kind="ExternalInput").ap()
    Wo = nc.dram_tensor("Wo", [DEPTH, D, D], F32, kind="ExternalInput").ap()
    bo = nc.dram_tensor("bo", [DEPTH, D], F32, kind="ExternalInput").ap()
    W1 = nc.dram_tensor("W1", [DEPTH, D, DFF], F32, kind="ExternalInput").ap()
    W2 = nc.dram_tensor("W2", [DEPTH, DFF, D], F32, kind="ExternalInput").ap()
    b2 = nc.dram_tensor("b2", [DEPTH, D], F32, kind="ExternalInput").ap()
    g1T = nc.dram_tensor("g1T", [DEPTH, 128, 8], F32, kind="ExternalInput").ap()
    b1nT = nc.dram_tensor("b1nT", [DEPTH, 128, 8], F32, kind="ExternalInput").ap()
    g2T = nc.dram_tensor("g2T", [DEPTH, 128, 8], F32, kind="ExternalInput").ap()
    b2nT = nc.dram_tensor("b2nT", [DEPTH, 128, 8], F32, kind="ExternalInput").ap()
    b1T = nc.dram_tensor("b1T", [DEPTH, 128, 32], F32, kind="ExternalInput").ap()
    ctrT = nc.dram_tensor("ctrT", [D, NE], F32, kind="ExternalInput").ap()
    ebb = nc.dram_tensor("ebb", [128, NE], F32, kind="ExternalInput").ap()
    Wsu = nc.dram_tensor("Wsu", [D, FE], F32, kind="ExternalInput").ap()
    Wsd = nc.dram_tensor("Wsd", [FE, D], F32, kind="ExternalInput").ap()
    Wup = nc.dram_tensor("Wup", [NE, D, FE], F32, kind="ExternalInput").ap()
    Wdn = nc.dram_tensor("Wdn", [NE, FE, D], F32, kind="ExternalInput").ap()
    cst = nc.dram_tensor("cst", [128, 672], BF16, kind="ExternalInput").ap()
    out = nc.dram_tensor("out", [T, D], F32, kind="ExternalOutput").ap()
    tap_outs = {}
    for tn, shape, dt in taps:
        tap_outs[tn] = nc.dram_tensor("tap_" + tn, shape, dt,
                                      kind="ExternalOutput").ap()

    with ExitStack() as stack:
        e = stack.enter_context
        x_sb = e(nc.sbuf_tensor("x_sb", [128, NT * 1024], F32))
        aT = e(nc.sbuf_tensor("aT", [128, 8 * T], BF16))
        bigB = e(nc.sbuf_tensor("bigB", [128, 8 * T], BF16))
        wA = e(nc.sbuf_tensor("wA", [128, 18432], BF16))
        qkvA = e(nc.sbuf_tensor("qkvA", [128, 12288], BF16))
        apre = e(nc.sbuf_tensor("apre", [128, 2048], BF16))
        stage = e(nc.sbuf_tensor("stage", [128, 1024], F32))
        smalls = e(nc.sbuf_tensor("smalls", [128, 1536], F32))
        constsB = e(nc.sbuf_tensor("constsB", [128, 672], BF16))
        nbuf = e(nc.sbuf_tensor("nbuf", [128, 256], BF16))
        brow = e(nc.sbuf_tensor("brow", [1, 2048], BF16))
        Pf = e(nc.psum_tensor("Pf", [128, 3072], F32))
        Pb = e(nc.psum_tensor("Pb", [128, 2048], BF16))

        P = Prog(nc, stack)
        te, act, dve, sp, gp = P.te, P.act, P.dve, P.sp, P.gp

        gbank = [Guard(P) for _ in range(8)]
        gstage = Guard(P)
        gapre = Guard(P)
        gaT = Guard(P)
        gslot = {}
        rg = {}

        def RG(name):
            if name not in rg:
                rg[name] = Guard(P)
            return rg[name]

        SREGS = tuple("st%d" % i for i in range(4))
        AREGS = tuple("ap%d" % i for i in range(8))

        def SG(i):
            return RG("st%d" % i)

        def apg(i):
            return RG("ap%d" % i)

        def stage_use_all(eng):
            gstage.use(eng)
            for n in SREGS:
                RG(n).use(eng)

        def apre_use_all(eng):
            gapre.use(eng)
            for n in AREGS:
                RG(n).use(eng)

        def apre_use_half(eng, half):
            gapre.use(eng)
            for i in range(4 * half, 4 * half + 4):
                apg(i).use(eng)

        def apre_done_half(eng, half):
            tok = eng.mark()
            gapre.done_tok(eng.name, tok)
            for i in range(4 * half, 4 * half + 4):
                apg(i).done_tok(eng.name, tok)

        def slot_guard(name):
            if name not in gslot:
                gslot[name] = Guard(P)
            return gslot[name]

        ident = constsB[:, 0:128]
        mask_wk = constsB[:, 128:384]
        mask_w0 = constsB[:, 384:512]
        ones_row = constsB[0:1, 512:640]

        def fbank(i, lo=0, hi=512):
            return Pf[:, i * 512 + lo:i * 512 + hi]

        def fpair(i, n=1024):
            return Pf[:, i * 512:i * 512 + n]

        def bbank(i, lo=0, hi=1024):
            return Pb[:, (i - 6) * 1024 + lo:(i - 6) * 1024 + hi]

        def sm(col, n=1):
            return smalls[:, col:col + n]

        def xt(t, lo=0, hi=1024):
            return x_sb[:, t * 1024 + lo:t * 1024 + hi]

        def aTv(dt, t0, t1):
            return aT[:, dt * T + t0:dt * T + t1]

        def oTv(et, t0, t1):
            return bigB[:, et * T + t0:et * T + t1]

        def wv(off, n):
            return wA[:, off:off + n]

        tok_cst = P.dma(sp, "x", constsB[:], cst)
        xr = xin.rearrange("(a p) d -> p a d", p=128)
        xv = x_sb[:].rearrange("p (a d) -> p a d", a=NT)
        tok_x = P.dma(sp, "x", xv, xr)
        tok_eb = P.dma(sp, "x", smalls[:, EB:EB + 64], ebb)

        gbrow = slot_guard("brow")

        def load_gb(i, deps=()):
            toks = [
                P.dma(sp, "x", smalls[:, GT1:GT1 + 8], g1T[i], deps=deps),
                P.dma(sp, "x", smalls[:, BT1:BT1 + 8], b1nT[i]),
                P.dma(sp, "x", smalls[:, GT2:GT2 + 8], g2T[i]),
                P.dma(sp, "x", smalls[:, BT2:BT2 + 8], b2nT[i]),
                P.dma(sp, "x", smalls[:, B1T:B1T + 32], b1T[i]),
            ]
            gbrow.use(gp)
            t6 = P.dma(gp, "w", brow[0:1, 0:1024], bo[i:i + 1, :])
            t7 = P.dma(gp, "w", brow[0:1, 1024:2048], b2[i:i + 1, :])
            gbrow.done_tok("gp", t7)
            return toks + [t6, t7]

        gb_tok = load_gb(0)
        ct_tok = P.dma(gp, "w",
                       wv(16384, 512).rearrange("p (t n) -> p t n", t=8),
                       ctrT.rearrange("(t p) e -> p t e", p=128))

        P.wait(dve, tok_x)
        for t in range(NT):
            dve.emit(nc.vector.reduce_sum(sm(SUMS + t), xt(t), axis=AX.X))
        sums = [dve.mark()] * NT

        xw_tok = [tok_x] * NT

        def emit_ln(gcol, bcol, sums_toks, inplace_nat=False, gb_toks=()):
            P.waits(dve, sums_toks)
            dve.emit(nc.vector.tensor_scalar(
                sm(NEGMU, NT), sm(SUMS, NT), -1.0 / D, None, op0=ALU.mult))
            negmu_tok = dve.mark()
            P.wait(act, negmu_tok)
            for t in range(NT):
                engv = act if t % 2 == 0 else dve
                if engv is act:
                    stage_use_all(engv)
                    P.wait(engv, xw_tok[t])
                    engv.emit(nc.scalar.activation(
                        stage[:], xt(t), AF.Square, bias=sm(NEGMU + t),
                        scale=1.0, accum_out=sm(VARSS + t)))
                    gstage.done(engv)
                else:
                    gbank[0].use(engv)
                    gbank[1].use(engv)
                    P.wait(engv, xw_tok[t])
                    P.wait(engv, negmu_tok)
                    engv.emit(nc.vector.scalar_tensor_tensor(
                        fpair(0), xt(t), sm(NEGMU + t), xt(t),
                        op0=ALU.add, op1=ALU.mult,
                        accum_out=sm(VARSS + t)))
                    gbank[0].done(engv)
                    gbank[1].done(engv)
            vs_tok = act.mark()
            P.wait(dve, vs_tok)
            P.selfbar(dve)
            dve.emit(nc.vector.tensor_scalar(
                sm(STD, NT), sm(VARSS, NT), 1.0 / D, 1e-5,
                op0=ALU.mult, op1=ALU.add))
            pre_tok = dve.mark()
            P.wait(act, pre_tok)
            act.emit(nc.scalar.activation(sm(STD, NT), sm(STD, NT), AF.Sqrt))
            std_tok = act.mark()
            P.wait(dve, std_tok)
            dve.emit(nc.vector.reciprocal(sm(RECR, NT), sm(STD, NT)))
            P.selfbar(dve)
            dve.emit(nc.vector.tensor_tensor(
                sm(MUR, NT), sm(NEGMU, NT), sm(RECR, NT), op=ALU.mult))
            mur_tok = dve.mark()

            aT_toks = []
            tr_prev = {6: None, 7: None}
            for b in range(NT):
                t = b
                pb = 6 + (b % 2)
                half = b % 2
                ap0 = half * 1024
                neng = act if b % 2 == 0 else dve
                apre_use_half(neng, half)
                P.wait(neng, mur_tok)
                P.wait(neng, xw_tok[t])
                if neng is act:
                    neng.emit(nc.scalar.activation(
                        apre[:, ap0:ap0 + 1024], xt(t), AF.Identity,
                        bias=sm(MUR + t), scale=sm(RECR + t)))
                else:
                    neng.emit(nc.vector.tensor_scalar(
                        apre[:, ap0:ap0 + 1024], xt(t),
                        sm(NEGMU + t), sm(RECR + t),
                        op0=ALU.add, op1=ALU.mult))
                if inplace_nat:
                    P.selfbar(neng)
                    if neng is act:
                        neng.emit(nc.scalar.activation(
                            xt(t), xt(t), AF.Identity,
                            bias=sm(MUR + t), scale=sm(RECR + t)))
                    else:
                        neng.emit(nc.vector.tensor_scalar(
                            xt(t), xt(t), sm(NEGMU + t), sm(RECR + t),
                            op0=ALU.add, op1=ALU.mult))
                    xw_tok[t] = neng.mark()
                apre_done_half(neng, half)
                norm_tok = neng.mark()
                g = gbank[pb]
                g.use(te)
                apre_use_half(te, half)
                P.wait(te, norm_tok)
                P.wait(te, tok_cst)
                for dt in range(8):
                    te.emit(nc.tensor.transpose(
                        bbank(pb, dt * 128, (dt + 1) * 128),
                        apre[:, ap0 + dt * 128:ap0 + (dt + 1) * 128],
                        ident))
                g.done(te)
                apre_done_half(te, half)
                tr_tok = te.mark()
                for dt in range(8):
                    eng = act if dt % 2 == 0 else dve
                    g.use(eng)
                    gaT.use(eng)
                    P.wait(eng, tr_tok)
                    P.waits(eng, gb_toks)
                    src = bbank(pb, dt * 128, dt * 128 + 128)
                    dst = aTv(dt, t * 128, (t + 1) * 128)
                    if gcol is None:
                        if eng is act:
                            eng.emit(nc.scalar.copy(dst, src))
                        else:
                            eng.emit(nc.vector.tensor_copy(dst, src))
                    else:
                        if eng is act:
                            eng.emit(nc.scalar.activation(
                                dst, src, AF.Identity,
                                bias=sm(bcol + dt), scale=sm(gcol + dt)))
                        else:
                            eng.emit(nc.vector.tensor_scalar(
                                dst, src, sm(gcol + dt), sm(bcol + dt),
                                op0=ALU.mult, op1=ALU.add))
                    g.done(eng)
                aT_toks.append(act.mark())
                aT_toks.append(dve.mark())
            return aT_toks

        WPAIR = {0: 14336, 1: 2048}

        def load_wpair(W, i, h0, slot=0):
            g = slot_guard("wpair%d" % slot)
            g.use(gp)
            src = W[i][:, h0 * 128:(h0 + 2) * 128].rearrange(
                "(t p) n -> p t n", p=128)
            dst = wv(WPAIR[slot], 2048).rearrange("p (t n) -> p t n", t=8)
            tok = P.dma(gp, "w", dst, src)
            g.done_tok("gp", tok)
            return (tok, slot)

        def load_whead(W, i, h, parity, which):
            g = slot_guard("qkv%d_%d" % (parity, which))
            g.use(gp)
            src = W[i][:, h * 128:(h + 1) * 128].rearrange(
                "(t p) n -> p t n", p=128)
            dst = wv(parity * 3072 + which * 1024, 1024).rearrange(
                "p (t n) -> p t n", t=8)
            tok = P.dma(gp, "w", dst, src)
            g.done_tok("gp", tok)
            return tok

        def load_wo(i):
            g = slot_guard("wo")
            g.use(gp)
            src = Wo[i].rearrange("(t p) n -> p t n", p=128)
            dst = wv(6144, 8192).rearrange("p (t n) -> p t n", t=8)
            tok = P.dma(gp, "w", dst, src)
            g.done_tok("gp", tok)
            return tok

        def load_up(src2d, parity):
            g = slot_guard("w1_%d" % parity)
            g.use(gp)
            src = src2d.rearrange("(t p) n -> p t n", p=128)
            dst = wv(parity * 4096, 4096).rearrange("p (t n) -> p t n", t=8)
            tok = P.dma(gp, "w", dst, src)
            g.done_tok("gp", tok)
            return tok

        def load_dn(src2d, parity, ttiles):
            g = slot_guard("w2_%d" % parity)
            g.use(gp)
            src = src2d.rearrange("(t p) n -> p t n", p=128)
            dst = wv(8192 + parity * 4096, 4096).rearrange(
                "p (t n) -> p t n", t=ttiles)
            tok = P.dma(gp, "w", dst, src)
            g.done_tok("gp", tok)
            return tok

        def emit_attention(i, aT_toks):
            def v2v(t, c0, c1):
                return qkvA[:, t * 256 + c0:t * 256 + c1]

            def k2v(t, c0, c1):
                return qkvA[:, 4096 + t * 256 + c0:4096 + t * 256 + c1]

            def kTv(p, t0, t1):
                return qkvA[:, 4096 + p * 2048 + t0:4096 + p * 2048 + t1]

            def qTv(p, t0, t1):
                return qkvA[:, 8192 + p * 2048 + t0:8192 + p * 2048 + t1]

            def drive(*gens):
                live = list(gens)
                while live:
                    for gen in list(live):
                        try:
                            next(gen)
                        except StopIteration:
                            live.remove(gen)

            pbctr = [0]

            def v2_gen(wtok_slot, v_prog):
                wtok, slot = wtok_slot
                base = WPAIR[slot]
                for t in range(NT):
                    pb = pbctr[0] % 2
                    pbctr[0] += 1
                    g = gbank[pb]
                    g.use(te)
                    if t == 0:
                        P.waits(te, aT_toks)
                        P.wait(te, wtok)
                    for dt in range(8):
                        te.emit(nc.tensor.matmul(
                            fbank(pb, 0, 256),
                            aTv(dt, t * 128, (t + 1) * 128),
                            wv(base + dt * 256, 256),
                            start=(dt == 0), stop=(dt == 7)))
                    g.done(te)
                    mm_tok = te.mark()
                    eng = act if t % 2 == 0 else dve
                    g.use(eng)
                    P.wait(eng, mm_tok)
                    if eng is act:
                        eng.emit(nc.scalar.copy(v2v(t, 0, 256),
                                                fbank(pb, 0, 256)))
                    else:
                        eng.emit(nc.vector.tensor_copy(v2v(t, 0, 256),
                                                       fbank(pb, 0, 256)))
                    g.done(eng)
                    v_prog.append(eng.mark())
                    yield
                slot_guard("wpair%d" % slot).done(te)
                gaT.done(te)
                v_prog.append(v_prog[-1])

            def pT_gen(wbase, dst_view, scale, wtok, wslot, prog):
                for c in range(4):
                    pb = pbctr[0] % 2
                    pbctr[0] += 1
                    g = gbank[pb]
                    g.use(te)
                    if c == 0:
                        P.waits(te, aT_toks)
                        P.wait(te, wtok)
                    for dt in range(8):
                        te.emit(nc.tensor.matmul(
                            fbank(pb),
                            wv(wbase + dt * 128, 128),
                            aTv(dt, c * 512, (c + 1) * 512),
                            start=(dt == 0), stop=(dt == 7)))
                    g.done(te)
                    mm_tok = te.mark()
                    eng = act if (c % 2 == 0 and scale == 1.0) else dve
                    g.use(eng)
                    P.wait(eng, mm_tok)
                    dst = dst_view(c * 512, (c + 1) * 512)
                    if scale != 1.0:
                        eng.emit(nc.vector.tensor_scalar(
                            dst, fbank(pb), scale, None, op0=ALU.mult))
                    elif eng is act:
                        eng.emit(nc.scalar.copy(dst, fbank(pb)))
                    else:
                        eng.emit(nc.vector.tensor_copy(dst, fbank(pb)))
                    g.done(eng)
                    prog.append(eng.mark())
                    yield
                wslot.done(te)
                gaT.done(te)

            def win_gen(h, v_prog, q_prog, k_prog):
                par = h % 2
                hh = par
                tpb = 6 + par
                if par == 0:
                    uslot = (0, 1)
                    nslot = 2
                    sslot = (0, 2)
                    atn, at_off, atreg = nbuf, 0, None
                else:
                    uslot = (4, 6)
                    nslot = 7
                    sslot = (1, 3)
                    atn, at_off, atreg = apre, 1280, apg(5)
                nreg = apg(nslot)
                nbase = nslot * 256
                ws_used = [None, None]
                state = None

                def front(w):
                    njj = 1 if w == 0 else 2
                    jt0 = max(w - 1, 0)
                    nj = njj * 128
                    dpb = 2 + par + 2 * (w % 2)
                    g = gbank[dpb]
                    g.use(te)
                    P.wait(te, q_prog[w // 4])
                    P.wait(te, k_prog[w // 4])
                    te.emit(nc.tensor.matmul(
                        fbank(dpb, 0, nj),
                        qTv(par, w * 128, (w + 1) * 128),
                        kTv(par, jt0 * 128, (w + 1) * 128),
                        start=True, stop=True))
                    g.done(te)
                    return te.mark(), dpb, njj, jt0, nj

                def front2(w, dots_tok, dpb, nj):
                    ws = WS0 + 3 * (par * 2 + w % 2)
                    ss = sslot[w % 2]
                    sreg = SG(ss)
                    sbase = ss * 256
                    us = uslot[w % 2]
                    ureg = apg(us)
                    ubase = us * 256
                    mask = mask_w0 if w == 0 else mask_wk
                    g = gbank[dpb]
                    g.use(dve)
                    gstage.use(dve)
                    sreg.use(dve)
                    P.wait(dve, dots_tok)
                    P.wait(dve, tok_cst)
                    P.wait(dve, ws_used[w % 2])
                    dve.emit(nc.vector.tensor_tensor(
                        stage[:, sbase:sbase + nj], fbank(dpb, 0, nj),
                        mask[:, 0:nj], op=ALU.add))
                    g.done(dve)
                    P.selfbar(dve)
                    dve.emit(nc.vector.reduce_max(
                        sm(ws), stage[:, sbase:sbase + nj], axis=AX.X,
                        negate=True))
                    sreg.done(dve)
                    nm_tok = dve.mark()
                    gapre.use(act)
                    sreg.use(act)
                    ureg.use(act)
                    P.wait(act, nm_tok)
                    act.emit(nc.scalar.activation(
                        apre[:, ubase:ubase + nj], stage[:, sbase:sbase + nj],
                        AF.Exp, bias=sm(ws), scale=1.0,
                        accum_out=sm(ws + 1)))
                    sreg.done(act)
                    ureg.done(act)
                    return act.mark(), ws, ureg, ubase

                def tail1(st):
                    (w, exp_tok, ws, ureg, ubase, njj, jt0, nj) = st
                    gapre.use(dve)
                    ureg.use(dve)
                    nreg.use(dve)
                    P.wait(dve, exp_tok)
                    dve.emit(nc.vector.reciprocal(sm(ws + 2), sm(ws + 1)))
                    P.selfbar(dve)
                    dve.emit(nc.vector.tensor_scalar(
                        apre[:, nbase:nbase + nj], apre[:, ubase:ubase + nj],
                        sm(ws + 2), None, op0=ALU.mult))
                    ureg.done(dve)
                    nreg.done(dve)
                    ws_used[w % 2] = dve.mark()
                    return ws_used[w % 2]

                def tail2(st, an_tok):
                    (w, exp_tok, ws, ureg, ubase, njj, jt0, nj) = st
                    gt = gbank[tpb]
                    gt.use(te)
                    gapre.use(te)
                    nreg.use(te)
                    P.wait(te, an_tok)
                    for jj in range(njj):
                        te.emit(nc.tensor.transpose(
                            bbank(tpb, jj * 128, (jj + 1) * 128),
                            apre[:, nbase + jj * 128:nbase + (jj + 1) * 128],
                            ident))
                    gt.done(te)
                    nreg.done(te)
                    trt_tok = te.mark()
                    eeng = dve if par == 0 else act
                    gt.use(eeng)
                    if atreg is not None:
                        atreg.use(eeng)
                    P.wait(eeng, trt_tok)
                    if eeng is dve:
                        eeng.emit(nc.vector.tensor_copy(
                            atn[:, at_off:at_off + nj], bbank(tpb, 0, nj)))
                    else:
                        eeng.emit(nc.scalar.copy(
                            atn[:, at_off:at_off + nj], bbank(tpb, 0, nj)))
                    gt.done(eeng)
                    if atreg is not None:
                        atreg.done(eeng)
                    return eeng.mark()

                def tail3(st, at_tok):
                    (w, exp_tok, ws, ureg, ubase, njj, jt0, nj) = st
                    obk = 2 + par + 2 * ((w + 1) % 2)
                    go = gbank[obk]
                    go.use(te)
                    if atreg is not None:
                        atreg.use(te)
                    P.wait(te, at_tok)
                    for jj in range(njj):
                        jw = jt0 + jj
                        P.wait(te, v_prog[jw])
                        te.emit(nc.tensor.matmul(
                            fbank(obk, 256, 384),
                            v2v(jw, hh * 128, (hh + 1) * 128),
                            atn[:, at_off + jj * 128:at_off + (jj + 1) * 128],
                            start=(jj == 0), stop=(jj == njj - 1)))
                    go.done(te)
                    if atreg is not None:
                        atreg.done(te)
                    o_tok = te.mark()
                    go.use(act)
                    P.wait(act, o_tok)
                    act.emit(nc.scalar.copy(
                        oTv(h, w * 128, (w + 1) * 128), fbank(obk, 256, 384)))
                    go.done(act)

                for w in range(NW):
                    while len(q_prog) <= w // 4 or len(k_prog) <= w // 4 or \
                            len(v_prog) <= w:
                        yield
                    dots_tok, dpb, njj, jt0, nj = front(w)
                    yield
                    exp_tok, ws, ureg, ubase = front2(w, dots_tok, dpb, nj)
                    yield
                    if state is not None:
                        an = tail1(state)
                        yield
                        at = tail2(state, an)
                        yield
                        tail3(state, at)
                        yield
                    state = (w, exp_tok, ws, ureg, ubase, njj, jt0, nj)
                an = tail1(state)
                yield
                at = tail2(state, an)
                yield
                tail3(state, at)

            wo_tok = None
            vp0 = load_wpair(Wv, i, 0, slot=0)
            wq0 = load_whead(Wq, i, 0, 0, 0)
            wk0 = load_whead(Wk, i, 0, 0, 1)
            wq1 = load_whead(Wq, i, 1, 1, 0)
            wk1 = load_whead(Wk, i, 1, 1, 1)
            wo_tok = load_wo(i)
            vpr, q0p, k0p, q1p, k1p = [], [], [], [], []
            gens = [
                v2_gen(vp0, vpr),
                pT_gen(0, lambda a, b: qTv(0, a, b), float(DH) ** -0.5,
                       wq0, slot_guard("qkv0_0"), q0p),
                pT_gen(1024, lambda a, b: kTv(0, a, b), 1.0,
                       wk0, slot_guard("qkv0_1"), k0p),
                pT_gen(3072, lambda a, b: qTv(1, a, b), float(DH) ** -0.5,
                       wq1, slot_guard("qkv1_0"), q1p),
                pT_gen(3072 + 1024, lambda a, b: kTv(1, a, b), 1.0,
                       wk1, slot_guard("qkv1_1"), k1p),
                win_gen(0, vpr, q0p, k0p),
                win_gen(1, vpr, q1p, k1p),
            ]
            drive(*gens)
            vp2 = load_wpair(Wv, i, 2, slot=0)
            wq2 = load_whead(Wq, i, 2, 0, 0)
            wk2 = load_whead(Wk, i, 2, 0, 1)
            wq3 = load_whead(Wq, i, 3, 1, 0)
            wk3 = load_whead(Wk, i, 3, 1, 1)
            vpr, q0p, k0p, q1p, k1p = [], [], [], [], []
            gens = [
                v2_gen(vp2, vpr),
                pT_gen(0, lambda a, b: qTv(0, a, b), float(DH) ** -0.5,
                       wq2, slot_guard("qkv0_0"), q0p),
                pT_gen(1024, lambda a, b: kTv(0, a, b), 1.0,
                       wk2, slot_guard("qkv0_1"), k0p),
                pT_gen(3072, lambda a, b: qTv(1, a, b), float(DH) ** -0.5,
                       wq3, slot_guard("qkv1_0"), q1p),
                pT_gen(3072 + 1024, lambda a, b: kTv(1, a, b), 1.0,
                       wk3, slot_guard("qkv1_1"), k1p),
                win_gen(2, vpr, q0p, k0p),
                win_gen(3, vpr, q1p, k1p),
            ]
            drive(*gens)
            lin_start = te.mark()
            slot_guard("wpair1").done_tok("te", lin_start)
            vpl = load_wpair(Wv, i, NLOC, slot=0)
            kpl = load_wpair(Wk, i, NLOC, slot=1)

            def k2_gen(kp_tok_pair, k_prog):
                kp_tok_v, kslot = kp_tok_pair
                kbase = WPAIR[kslot]
                gk0 = slot_guard("qkv0_1")
                gk1 = slot_guard("qkv1_1")
                e0 = apg(0)
                e1 = apg(1)
                for t in range(NT):
                    pb = 2
                    g = gbank[pb]
                    g.use(te)
                    if t == 0:
                        P.waits(te, aT_toks)
                        P.wait(te, kp_tok_v)
                        gk0.use(te)
                        gk1.use(te)
                    for dt in range(8):
                        te.emit(nc.tensor.matmul(
                            fbank(pb, 0, 256),
                            aTv(dt, t * 128, (t + 1) * 128),
                            wv(kbase + dt * 256, 256),
                            start=(dt == 0), stop=(dt == 7)))
                    g.done(te)
                    mm_tok = te.mark()
                    yield
                    g.use(dve)
                    e0.use(dve)
                    P.wait(dve, mm_tok)
                    dve.emit(nc.vector.tensor_scalar(
                        apre[:, 0:256], fbank(pb, 0, 256), 0.0, None,
                        op0=ALU.min))
                    e0.done(dve)
                    mn_tok = dve.mark()
                    e0.use(act)
                    e1.use(act)
                    P.wait(act, mn_tok)
                    act.emit(nc.scalar.activation(
                        apre[:, 256:512], apre[:, 0:256], AF.Exp))
                    e0.done(act)
                    e1.done(act)
                    ex_tok = act.mark()
                    g.use(dve)
                    e1.use(dve)
                    P.wait(dve, ex_tok)
                    dve.emit(nc.vector.scalar_tensor_tensor(
                        k2v(t, 0, 256), fbank(pb, 0, 256), 0.0,
                        apre[:, 256:512], op0=ALU.max, op1=ALU.add))
                    g.done(dve)
                    e1.done(dve)
                    k_prog.append(dve.mark())
                    yield
                gk0.done_tok("dve", k_prog[-1])
                gk1.done_tok("dve", k_prog[-1])
                slot_guard("wpair%d" % kslot).done(te)
                gaT.done(te)

            def qgen(hh, qp_list, q_prog):
                while not qp_list:
                    yield
                qp_tok, qslot_i = qp_list[0]
                qbase = WPAIR[qslot_i]
                par = hh
                pb = hh
                tpb = 6 + hh
                sreg = SG(hh)
                qreg = apg(4)
                sbase = hh * 256
                ap_off = 1024 + hh * 128
                gq = slot_guard("qkv%d_0" % par)
                g = gbank[pb]
                gt = gbank[tpb]
                qa_last = None
                for t in range(NT):
                    g.use(te)
                    if t == 0:
                        P.wait(te, qp_tok)
                        gq.use(te)
                    for dt in range(8):
                        te.emit(nc.tensor.matmul(
                            fbank(pb, 0, 128),
                            aTv(dt, t * 128, (t + 1) * 128),
                            wv(qbase + dt * 256 + hh * 128, 128),
                            start=(dt == 0), stop=(dt == 7)))
                    g.done(te)
                    mm_tok = te.mark()
                    yield
                    qs = QS0 + 3 * (hh * 2 + t % 2)
                    g.use(dve)
                    P.wait(dve, mm_tok)
                    dve.emit(nc.vector.reduce_max(
                        sm(qs), fbank(pb, 0, 128), axis=AX.X, negate=True))
                    nm_tok = dve.mark()
                    g.use(act)
                    gstage.use(act)
                    sreg.use(act)
                    P.wait(act, nm_tok)
                    act.emit(nc.scalar.activation(
                        stage[:, sbase:sbase + 128], fbank(pb, 0, 128),
                        AF.Exp, bias=sm(qs), scale=1.0, accum_out=sm(qs + 1)))
                    g.done(act)
                    sreg.done(act)
                    ex_tok = act.mark()
                    yield
                    P.wait(dve, ex_tok)
                    dve.emit(nc.vector.reciprocal(sm(qs + 2), sm(qs + 1)))
                    rc_tok = dve.mark()
                    gapre.use(act)
                    qreg.use(act)
                    sreg.use(act)
                    P.wait(act, rc_tok)
                    P.wait(act, qa_last)
                    act.emit(nc.scalar.activation(
                        apre[:, ap_off:ap_off + 128],
                        stage[:, sbase:sbase + 128],
                        AF.Copy, scale=sm(qs + 2)))
                    sreg.done(act)
                    qreg.done(act)
                    qn_tok = act.mark()
                    gt.use(te)
                    qreg.use(te)
                    P.wait(te, qn_tok)
                    te.emit(nc.tensor.transpose(
                        bbank(tpb, 0, 128),
                        apre[:, ap_off:ap_off + 128], ident))
                    gt.done(te)
                    qreg.done(te)
                    qa_last = te.mark()
                    gt.use(dve)
                    P.wait(dve, qa_last)
                    dve.emit(nc.vector.tensor_copy(
                        qTv(par, t * 128, (t + 1) * 128),
                        bbank(tpb, 0, 128)))
                    gt.done(dve)
                    q_prog.append(dve.mark())
                    yield
                gq.done_tok("dve", dve.mark())
                slot_guard("wpair%d" % qslot_i).done(te)
                gaT.done(te)

            def buck_gen(h, hh, v_prog, k_prog, q_prog):
                sbk = 4 + hh
                obk = 3
                oof = hh * 64
                gS = gbank[sbk]
                gO = gbank[obk]
                er = SG(2) if hh == 0 else SG(3)
                nreg = apg(6) if hh == 0 else apg(7)
                s0 = 512 + hh * 256
                nb_base = 1536 + hh * 256
                dve.emit(nc.vector.memset(oTv(h, 0, 64), 0.0))
                gstage.use(dve)
                er.use(dve)
                dve.emit(nc.vector.memset(stage[:, s0:s0 + 129], 0.0))
                er.done(dve)
                n_tok = None
                lr = LRC + 2 * hh
                for u in range(NB):
                    tt = u // 2
                    r0 = (u % 2) * 64
                    if u < NB - 1:
                        while len(k_prog) <= tt or len(v_prog) <= tt:
                            yield
                        gS.use(te)
                        P.wait(te, k_prog[tt])
                        P.wait(te, v_prog[tt])
                        klhs = k2v(tt, hh * 128,
                                   (hh + 1) * 128)[r0:r0 + 64, :]
                        te.emit(nc.tensor.matmul(
                            fbank(sbk, 0, 128), klhs,
                            v2v(tt, hh * 128, (hh + 1) * 128)[r0:r0 + 64, :],
                            start=True, stop=False))
                        te.emit(nc.tensor.matmul(
                            fbank(sbk, 128, 129), klhs,
                            constsB[r0:r0 + 64, 640:641],
                            start=False, stop=True))
                        gS.done(te)
                        s_tok = te.mark()
                    if u > 0:
                        while len(q_prog) <= u // 2:
                            yield
                        gO.use(te)
                        nreg.use(te)
                        P.wait(te, n_tok)
                        P.wait(te, q_prog[u // 2])
                        te.emit(nc.tensor.matmul(
                            fbank(obk, oof, oof + 64),
                            apre[:, nb_base + ((u - 1) % 2) * 128:
                                 nb_base + ((u - 1) % 2 + 1) * 128],
                            qTv(hh, u * 64, (u + 1) * 64),
                            start=True, stop=True))
                        gO.done(te)
                        nreg.done(te)
                        ou_tok = te.mark()
                        gO.use(act)
                        P.wait(act, ou_tok)
                        act.emit(nc.scalar.copy(
                            oTv(h, u * 64, (u + 1) * 64),
                            fbank(obk, oof, oof + 64)))
                        gO.done(act)
                    if u == NB - 1:
                        break
                    yield
                    gS.use(dve)
                    gstage.use(dve)
                    er.use(dve)
                    P.wait(dve, s_tok)
                    dve.emit(nc.vector.tensor_tensor(
                        stage[:, s0:s0 + 129], fbank(sbk, 0, 129),
                        stage[:, s0:s0 + 129], op=ALU.add))
                    P.selfbar(dve)
                    dve.emit(nc.vector.tensor_scalar(
                        sm(lr), stage[:, s0 + 128:s0 + 129], 1e-6, None,
                        op0=ALU.add))
                    P.selfbar(dve)
                    dve.emit(nc.vector.reciprocal(sm(lr + 1), sm(lr)))
                    gS.done(dve)
                    er.done(dve)
                    rc_tok = dve.mark()
                    gstage.use(act)
                    er.use(act)
                    nreg.use(act)
                    P.wait(act, rc_tok)
                    act.emit(nc.scalar.activation(
                        apre[:, nb_base + (u % 2) * 128:
                             nb_base + (u % 2 + 1) * 128],
                        stage[:, s0:s0 + 128], AF.Copy, scale=sm(lr + 1)))
                    er.done(act)
                    nreg.done(act)
                    n_tok = act.mark()
                    yield

            def q_loader(v_prog, h0, slot, qp_list):
                while len(v_prog) < NT + 1:
                    yield
                qp_list.append(load_wpair(Wq, i, h0, slot=slot))

            vpr, kpr, qp0, qp1 = [], [], [], []
            qpl = []
            gens = [
                v2_gen(vpl, vpr),
                q_loader(vpr, NLOC, 0, qpl),
                k2_gen(kpl, kpr),
                qgen(0, qpl, qp0), qgen(1, qpl, qp1),
                buck_gen(NLOC, 0, vpr, kpr, qp0),
                buck_gen(NLOC + 1, 1, vpr, kpr, qp1),
            ]
            drive(*gens)
            vpl1 = load_wpair(Wv, i, NLOC + 2, slot=1)
            kpl1 = load_wpair(Wk, i, NLOC + 2, slot=0)
            vpr, kpr, qp0, qp1 = [], [], [], []
            qpl1 = []
            gens = [
                v2_gen(vpl1, vpr),
                q_loader(vpr, NLOC + 2, 1, qpl1),
                k2_gen(kpl1, kpr),
                qgen(0, qpl1, qp0), qgen(1, qpl1, qp1),
                buck_gen(NLOC + 2, 0, vpr, kpr, qp0),
                buck_gen(NLOC + 3, 1, vpr, kpr, qp1),
            ]
            drive(*gens)
            oT_toks = [act.mark(), dve.mark()]

            gwo = slot_guard("wo")
            res_toks = []
            for t in range(NT):
                pair = (t % 2) * 2
                g0, g1 = gbank[pair], gbank[pair + 1]
                g0.use(te)
                g1.use(te)
                if t == 0:
                    P.waits(te, oT_toks)
                    P.wait(te, wo_tok)
                    P.wait(te, gb_tok[5])
                    gwo.use(te)
                    gbrow.use(te)
                for half in range(2):
                    pbk = pair + half
                    for et in range(8):
                        te.emit(nc.tensor.matmul(
                            fbank(pbk),
                            oTv(et, t * 128, (t + 1) * 128),
                            wv(6144 + et * 1024 + half * 512, 512),
                            start=(et == 0), stop=False))
                    te.emit(nc.tensor.matmul(
                        fbank(pbk), ones_row,
                        brow[0:1, half * 512:(half + 1) * 512],
                        start=False, stop=True))
                g0.done(te)
                g1.done(te)
                mm_tok = te.mark()
                g0.use(dve)
                g1.use(dve)
                P.wait(dve, mm_tok)
                P.wait(dve, xw_tok[t])
                dve.emit(nc.vector.scalar_tensor_tensor(
                    xt(t), fpair(pair), 1.0, xt(t),
                    op0=ALU.mult, op1=ALU.add, accum_out=sm(SUMS + t)))
                g0.done(dve)
                g1.done(dve)
                xw_tok[t] = dve.mark()
                res_toks.append(xw_tok[t])
            gwo.done(te)
            gbrow.done(te)
            return res_toks

        def emit_ffn(i, aT_toks):
            res_toks = [None] * NT
            wtoks = {0: (load_up(W1[i][:, 0:512], 0),
                         load_dn(W2[i][0:512, :], 0, 4))}
            for c in range(8):
                par = c % 2
                if c + 1 < 8:
                    wtoks[c + 1] = (
                        load_up(W1[i][:, (c + 1) * 512:(c + 2) * 512], 1 - par),
                        load_dn(W2[i][(c + 1) * 512:(c + 2) * 512, :],
                                1 - par, 4))
                w1t, w2t = wtoks.pop(c)
                gw1 = slot_guard("w1_%d" % par)
                gw2 = slot_guard("w2_%d" % par)
                fft_toks = []
                for fs in range(4):
                    for tc in range(4):
                        pb = (fs * 4 + tc) % 2
                        g = gbank[pb]
                        g.use(te)
                        if fs == 0 and tc == 0:
                            P.waits(te, aT_toks)
                            P.wait(te, w1t)
                            gw1.use(te)
                        for dt in range(8):
                            te.emit(nc.tensor.matmul(
                                fbank(pb),
                                wv(par * 4096 + dt * 512 + fs * 128, 128),
                                aTv(dt, tc * 512, (tc + 1) * 512),
                                start=(dt == 0), stop=(dt == 7)))
                        g.done(te)
                        mm_tok = te.mark()
                        g.use(act)
                        P.wait(act, mm_tok)
                        if fs == 0 and tc == 0:
                            P.wait(act, gb_tok[4])
                        act.emit(nc.scalar.activation(
                            bigB[:, par * 8192 + fs * 2048 + tc * 512:
                                 par * 8192 + fs * 2048 + (tc + 1) * 512],
                            fbank(pb), AF.Gelu,
                            bias=sm(B1T + c * 4 + fs), scale=1.0))
                        g.done(act)
                        fft_toks.append(act.mark())
                gw1.done(te)
                gaT.done(te)
                for t in range(NT):
                    pair = 2 + (t % 2) * 2
                    g0, g1 = gbank[pair], gbank[pair + 1]
                    g0.use(te)
                    g1.use(te)
                    if t == 0:
                        P.waits(te, fft_toks)
                        P.wait(te, w2t)
                        gw2.use(te)
                        if c == 7:
                            P.wait(te, gb_tok[6])
                            gbrow.use(te)
                    for half in range(2):
                        pbk = pair + half
                        for fs in range(4):
                            te.emit(nc.tensor.matmul(
                                fbank(pbk),
                                bigB[:, par * 8192 + fs * 2048 + t * 128:
                                     par * 8192 + fs * 2048 + (t + 1) * 128],
                                wv(8192 + par * 4096 + fs * 1024 + half * 512,
                                   512),
                                start=(fs == 0),
                                stop=(fs == 3 and c != 7)))
                        if c == 7:
                            te.emit(nc.tensor.matmul(
                                fbank(pbk), ones_row,
                                brow[0:1, 1024 + half * 512:
                                     1024 + (half + 1) * 512],
                                start=False, stop=True))
                    g0.done(te)
                    g1.done(te)
                    mm_tok = te.mark()
                    g0.use(dve)
                    g1.use(dve)
                    P.wait(dve, mm_tok)
                    P.wait(dve, xw_tok[t])
                    if c == 7:
                        dve.emit(nc.vector.scalar_tensor_tensor(
                            xt(t), fpair(pair), 1.0, xt(t),
                            op0=ALU.mult, op1=ALU.add,
                            accum_out=sm(SUMS + t)))
                    else:
                        dve.emit(nc.vector.tensor_tensor(
                            xt(t), fpair(pair), xt(t), op=ALU.add))
                    g0.done(dve)
                    g1.done(dve)
                    xw_tok[t] = dve.mark()
                    if c == 7:
                        res_toks[t] = xw_tok[t]
                gw2.done(te)
                if c == 7:
                    gbrow.done(te)
            return res_toks

        stopped = False
        for i in range(n_layers):
            P.wait(gp, te.mark())
            aT_toks = emit_ln(GT1, BT1, sums, gb_toks=gb_tok[0:2])
            if stop_after == "ln1" and i == n_layers - 1:
                stopped = True
                break
            sums = emit_attention(i, aT_toks)
            if stop_after == "attn" and i == n_layers - 1:
                stopped = True
                break
            P.wait(gp, te.mark())
            aT_toks = emit_ln(GT2, BT2, sums, gb_toks=gb_tok[2:4])
            if stop_after == "ln2" and i == n_layers - 1:
                stopped = True
                break
            sums = emit_ffn(i, aT_toks)
            if i + 1 < n_layers:
                gb_tok = load_gb(i + 1, deps=[act.mark(), dve.mark()])
        if stopped:
            do_top = False

        if do_top:
            if do_moe:
                def eload(eix, parity):
                    if eix == -1:
                        return (load_up(Wsu, parity), load_dn(Wsd, parity, 4))
                    return (load_up(Wup[eix], parity),
                            load_dn(Wdn[eix], parity, 4))

                ex_list = [-1] + list(range(n_experts))
                P.wait(gp, te.mark())
                etoks = {-1: eload(-1, 0)}
                if len(ex_list) > 1:
                    etoks[0] = eload(0, 1)
            for t in range(NT):
                stage_use_all(sp)
                x0t = P.dma(sp, "x", stage[:], xin[t * 128:(t + 1) * 128, :])
                gstage.done_tok("sp", x0t)
                stage_use_all(dve)
                P.wait(dve, xw_tok[t])
                dve.emit(nc.vector.scalar_tensor_tensor(
                    xt(t), stage[:], 1.0, xt(t), op0=ALU.mult, op1=ALU.add,
                    accum_out=sm(SUMS + t)))
                gstage.done(dve)
                xw_tok[t] = dve.mark()
            sums = [xw_tok[t] for t in range(NT)]
            aT_toks = emit_ln(None, None, sums, inplace_nat=True)

            if do_moe:
                cw_toks = []
                for t in range(NT):
                    pb = 2 + t % 2
                    g = gbank[pb]
                    g.use(te)
                    if t == 0:
                        P.waits(te, aT_toks)
                        P.wait(te, ct_tok)
                    for dt in range(8):
                        te.emit(nc.tensor.matmul(
                            fbank(pb, 0, 64),
                            aTv(dt, t * 128, (t + 1) * 128),
                            wv(16384 + dt * 64, 64),
                            start=(dt == 0), stop=(dt == 7)))
                    g.done(te)
                    aff_tok = te.mark()
                    c0 = TMP
                    g.use(dve)
                    P.wait(dve, aff_tok)
                    if t == 0:
                        P.wait(dve, tok_eb)
                    dve.emit(nc.vector.tensor_tensor(
                        sm(c0, 64), fbank(pb, 0, 64), smalls[:, EB:EB + 64],
                        op=ALU.add))
                    P.selfbar(dve)
                    dve.emit(nc.vector.max(sm(c0 + 64, 8), sm(c0, 64)))
                    P.selfbar(dve)
                    dve.emit(nc.vector.tensor_scalar(
                        sm(c0 + 72, 64), sm(c0, 64), sm(c0 + 71), None,
                        op0=ALU.is_ge))
                    b8_tok = dve.mark()
                    g.use(act)
                    P.wait(act, b8_tok)
                    act.emit(nc.scalar.activation(
                        sm(c0 + 140, 64), fbank(pb, 0, 64), AF.Sigmoid))
                    g.done(act)
                    sg_tok = act.mark()
                    g.done(dve)
                    P.wait(dve, sg_tok)
                    dve.emit(nc.vector.scalar_tensor_tensor(
                        sm(c0 + 140, 64), sm(c0 + 72, 64), 1.0,
                        sm(c0 + 140, 64), op0=ALU.mult, op1=ALU.mult,
                        accum_out=sm(c0 + 137)))
                    P.selfbar(dve)
                    dve.emit(nc.vector.tensor_scalar(
                        sm(c0 + 138), sm(c0 + 137), 1e-8, None, op0=ALU.add))
                    P.selfbar(dve)
                    dve.emit(nc.vector.reciprocal(sm(c0 + 139), sm(c0 + 138)))
                    P.selfbar(dve)
                    dve.emit(nc.vector.tensor_scalar(
                        sm(CW + t * 64, 64), sm(c0 + 140, 64), sm(c0 + 139),
                        None, op0=ALU.mult))
                    cw_toks.append(dve.mark())

                for idx, eix in enumerate(ex_list):
                    par = idx % 2
                    w_up_tok, w_dn_tok = etoks.pop(eix)
                    nidx = idx + 1
                    if nidx < len(ex_list) and ex_list[nidx] not in etoks:
                        etoks[ex_list[nidx]] = eload(ex_list[nidx], 1 - par)
                    gw1 = slot_guard("w1_%d" % par)
                    gw2 = slot_guard("w2_%d" % par)
                    hT_toks = []
                    for fs in range(4):
                        for tc in range(4):
                            pb = (fs * 4 + tc) % 2
                            g = gbank[pb]
                            g.use(te)
                            if fs == 0 and tc == 0:
                                P.waits(te, aT_toks)
                                P.wait(te, w_up_tok)
                                gw1.use(te)
                            for dt in range(8):
                                te.emit(nc.tensor.matmul(
                                    fbank(pb),
                                    wv(par * 4096 + dt * 512 + fs * 128, 128),
                                    aTv(dt, tc * 512, (tc + 1) * 512),
                                    start=(dt == 0), stop=(dt == 7)))
                            g.done(te)
                            mm_tok = te.mark()
                            g.use(act)
                            P.wait(act, mm_tok)
                            act.emit(nc.scalar.activation(
                                bigB[:, par * 8192 + fs * 2048 + tc * 512:
                                     par * 8192 + fs * 2048 + (tc + 1) * 512],
                                fbank(pb), AF.Silu))
                            g.done(act)
                            hT_toks.append(act.mark())
                    gw1.done(te)
                    gaT.done(te)
                    last = idx == len(ex_list) - 1
                    for t in range(NT):
                        pair = 2 + (t % 2) * 2
                        g0, g1 = gbank[pair], gbank[pair + 1]
                        g0.use(te)
                        g1.use(te)
                        if t == 0:
                            P.waits(te, hT_toks)
                            P.wait(te, w_dn_tok)
                            gw2.use(te)
                        for half in range(2):
                            pbk = pair + half
                            for fs in range(4):
                                te.emit(nc.tensor.matmul(
                                    fbank(pbk),
                                    bigB[:, par * 8192 + fs * 2048 + t * 128:
                                         par * 8192 + fs * 2048 +
                                         (t + 1) * 128],
                                    wv(8192 + par * 4096 + fs * 1024 +
                                       half * 512, 512),
                                    start=(fs == 0), stop=(fs == 3)))
                        g0.done(te)
                        g1.done(te)
                        mm_tok = te.mark()
                        g0.use(dve)
                        g1.use(dve)
                        P.wait(dve, mm_tok)
                        P.wait(dve, xw_tok[t])
                        if eix == -1:
                            dve.emit(nc.vector.tensor_tensor(
                                xt(t), fpair(pair), xt(t), op=ALU.add))
                        else:
                            P.wait(dve, cw_toks[t])
                            kw = {}
                            if last:
                                kw = dict(accum_out=sm(SUMS + t))
                            dve.emit(nc.vector.scalar_tensor_tensor(
                                xt(t), fpair(pair), sm(CW + t * 64 + eix),
                                xt(t), op0=ALU.mult, op1=ALU.add, **kw))
                        g0.done(dve)
                        g1.done(dve)
                        xw_tok[t] = dve.mark()
                    gw2.done(te)
                sums = [xw_tok[t] for t in range(NT)]
                if n_experts == 0:
                    for t in range(NT):
                        P.wait(dve, xw_tok[t])
                        dve.emit(nc.vector.reduce_sum(sm(SUMS + t), xt(t),
                                                      axis=AX.X))
                    sums = [dve.mark()] * NT

            if not do_moe:
                for t in range(NT):
                    P.wait(dve, xw_tok[t])
                    dve.emit(nc.vector.reduce_sum(sm(SUMS + t), xt(t),
                                                  axis=AX.X))
                sums = [dve.mark()] * NT
            P.waits(dve, sums)
            dve.emit(nc.vector.tensor_scalar(
                sm(NEGMU, NT), sm(SUMS, NT), -1.0 / D, None, op0=ALU.mult))
            negmu_tok = dve.mark()
            P.wait(act, negmu_tok)
            for t in range(NT):
                engv = act if t % 2 == 0 else dve
                if engv is act:
                    stage_use_all(engv)
                    P.wait(engv, xw_tok[t])
                    engv.emit(nc.scalar.activation(
                        stage[:], xt(t), AF.Square, bias=sm(NEGMU + t),
                        scale=1.0, accum_out=sm(VARSS + t)))
                    gstage.done(engv)
                else:
                    gbank[0].use(engv)
                    gbank[1].use(engv)
                    P.wait(engv, xw_tok[t])
                    P.wait(engv, negmu_tok)
                    engv.emit(nc.vector.scalar_tensor_tensor(
                        fpair(0), xt(t), sm(NEGMU + t), xt(t),
                        op0=ALU.add, op1=ALU.mult,
                        accum_out=sm(VARSS + t)))
                    gbank[0].done(engv)
                    gbank[1].done(engv)
            vs_tok = act.mark()
            P.wait(dve, vs_tok)
            P.selfbar(dve)
            dve.emit(nc.vector.tensor_scalar(
                sm(STD, NT), sm(VARSS, NT), 1.0 / D, 1e-5,
                op0=ALU.mult, op1=ALU.add))
            pre_tok = dve.mark()
            P.wait(act, pre_tok)
            act.emit(nc.scalar.activation(sm(STD, NT), sm(STD, NT), AF.Sqrt))
            std_tok = act.mark()
            P.wait(dve, std_tok)
            dve.emit(nc.vector.reciprocal(sm(RECR, NT), sm(STD, NT)))
            P.selfbar(dve)
            dve.emit(nc.vector.tensor_tensor(
                sm(MUR, NT), sm(NEGMU, NT), sm(RECR, NT), op=ALU.mult))
            mur_tok = dve.mark()
            P.wait(act, mur_tok)
            for t in range(NT):
                stage_use_all(act)
                act.emit(nc.scalar.activation(
                    stage[:], xt(t), AF.Identity, bias=sm(MUR + t),
                    scale=sm(RECR + t)))
                nt_tok = act.mark()
                st_tok = P.dma(sp, "o", out[t * 128:(t + 1) * 128, :],
                               stage[:], deps=[nt_tok])
                gstage.done_tok("sp", st_tok)
        else:
            for t in range(NT):
                P.wait(sp, xw_tok[t])
                P.dma(sp, "o", out[t * 128:(t + 1) * 128, :], xt(t))

        for tn, shape, dt in taps:
            ap = tap_outs[tn]
            for eng in (act, dve, te):
                tk = eng.mark()
                if tk is not None:
                    P.wait(sp, tk)
            if tn == "aT":
                P.dma(sp, "o", ap, aT[:])
            elif tn == "oT":
                P.dma(sp, "o", ap, bigB[:])
            elif tn == "x":
                xvv = ap.rearrange("(a p) d -> p a d", p=128)
                P.dma(sp, "o", xvv, xv)
            elif tn == "qkvA":
                P.dma(sp, "o", ap, qkvA[:])
            elif tn == "smalls":
                P.dma(sp, "o", ap, smalls[:])

        for kind in ("w", "x", "o"):
            pool = P.pools[kind]
            for sem, uses in zip(pool["sems"], pool["uses"]):
                if uses:
                    sp.e.wait_ge(sem, 16 * uses)
    return nc


def _make_consts():
    bf = ml_dtypes.bfloat16
    c = np.zeros((128, 672), dtype=bf)
    c[:, 0:128] = np.eye(128, dtype=np.float32).astype(bf)
    m = np.zeros((128, 256), dtype=np.float32)
    m[:, 128:256] = np.triu(np.full((128, 128), -1e30, np.float32), k=1)
    c[:, 128:384] = m.astype(bf)
    c[:, 384:512] = np.triu(np.full((128, 128), -1e30, np.float32),
                            k=1).astype(bf)
    c[0, 512:640] = bf(1.0)
    c[:, 640] = bf(1.0)
    return c


def _prep_inputs(inputs):
    f32 = np.float32
    g = {k: np.asarray(v) for k, v in inputs.items()}
    com = {}
    com["Wq"] = np.ascontiguousarray(g["L0_attn_Wq"], f32)
    com["Wk"] = np.ascontiguousarray(g["L0_attn_Wk"], f32)
    com["Wv"] = np.ascontiguousarray(g["L0_attn_Wv"], f32)
    com["Wo"] = np.ascontiguousarray(g["L0_attn_Wo"], f32)
    com["bo"] = np.ascontiguousarray(g["L0_attn_bo"], f32)
    com["W1"] = np.ascontiguousarray(g["L0_attn_W1"], f32)
    com["W2"] = np.ascontiguousarray(g["L0_attn_W2"], f32)
    com["b2"] = np.ascontiguousarray(g["L0_attn_b2"], f32)
    com["g1T"] = np.ascontiguousarray(
        g["L0_attn_ln1_g"].reshape(DEPTH, 8, 128).transpose(0, 2, 1), f32)
    com["b1nT"] = np.ascontiguousarray(
        g["L0_attn_ln1_b"].reshape(DEPTH, 8, 128).transpose(0, 2, 1), f32)
    com["g2T"] = np.ascontiguousarray(
        g["L0_attn_ln2_g"].reshape(DEPTH, 8, 128).transpose(0, 2, 1), f32)
    com["b2nT"] = np.ascontiguousarray(
        g["L0_attn_ln2_b"].reshape(DEPTH, 8, 128).transpose(0, 2, 1), f32)
    com["b1T"] = np.ascontiguousarray(
        g["L0_attn_b1"].reshape(DEPTH, 32, 128).transpose(0, 2, 1), f32)
    com["ctrT"] = np.ascontiguousarray(g["L0_ffn_centroids"].T, f32)
    com["ebb"] = np.ascontiguousarray(
        np.broadcast_to(g["L0_ffn_expert_biases"], (128, NE)), f32)
    com["Wsu"] = np.ascontiguousarray(g["L0_ffn_Ws_up"], f32)
    com["Wsd"] = np.ascontiguousarray(g["L0_ffn_Ws_down"], f32)
    com["Wup"] = np.ascontiguousarray(g["L0_ffn_W_up"], f32)
    com["Wdn"] = np.ascontiguousarray(g["L0_ffn_W_down"], f32)
    com["cst"] = _make_consts()
    return g, com


def kernel(**inputs):
    g, com = _prep_inputs(inputs)
    x_full = np.ascontiguousarray(np.asarray(g["L0_attn_x"], np.float32))
    nc = build_program()
    in_maps = []
    for i in range(8):
        m = dict(com)
        m["x"] = x_full[i]
        in_maps.append(m)
    res = run_bass_kernel_spmd(nc, in_maps, core_ids=list(range(8)))
    outs = [r["out"] for r in res.results]
    return np.stack(outs, axis=0).astype(np.float32)


# version 12
# speedup vs baseline: 1.0694x; 1.0694x over previous
import numpy as np
import ml_dtypes
from contextlib import ExitStack

import concourse.bass as bass
import concourse.mybir as mybir
from concourse.bass_utils import run_bass_kernel_spmd

F32 = mybir.dt.float32
BF16 = mybir.dt.bfloat16
AF = mybir.ActivationFunctionType
ALU = mybir.AluOpType
AX = mybir.AxisListType

T = 2048
D = 1024
NT = T // 128
HEADS = 8
DH = 128
NLOC = 4
NW = 16
NB = 32
DEPTH = 4
DFF = 4096
NE = 64
FE = 512

SUMS, NEGMU, VARSS, STD, RECR, MUR = 0, 16, 32, 48, 64, 80
WS0 = 96
QS0 = 108
LRC = 120
GT1, BT1, GT2, BT2 = 128, 136, 144, 152
B1T = 160
EB = 192
CW = 256
TMP = CW + NT * 64


class Eng:
    def __init__(self, bass_eng, sem, name):
        self.e = bass_eng
        self.sem = sem
        self.name = name
        self.count = 0
        self.waited = {}
        self.last_inst = None
        self.last_tok = None

    def mark(self):
        if self.last_tok is not None:
            return self.last_tok
        if self.last_inst is None:
            return None
        try:
            self.last_inst.then_inc(self.sem, 1)
        except Exception:
            self.e.maybe_drain_then_inc((self.sem, 1))
        self.count += 1
        self.last_tok = (self.sem, self.count, self)
        return self.last_tok

    def emit(self, inst):
        self.last_inst = inst
        self.last_tok = None
        return inst


class Prog:
    def __init__(self, nc, stack):
        self.nc = nc
        e = stack.enter_context
        self.te = Eng(nc.tensor, e(nc.semaphore("s_te")), "te")
        self.act = Eng(nc.scalar, e(nc.semaphore("s_act")), "act")
        self.dve = Eng(nc.vector, e(nc.semaphore("s_dve")), "dve")
        self.sp = Eng(nc.sync, e(nc.semaphore("s_sp")), "sp")
        self.gp = Eng(nc.gpsimd, e(nc.semaphore("s_gp")), "gp")
        self.pools = {}
        for kind, n in (("w", 10), ("x", 8), ("o", 4)):
            self.pools[kind] = {
                "sems": [e(nc.semaphore("s_%sdma%d" % (kind, j)))
                         for j in range(n)],
                "uses": [0] * n,
                "i": 0,
            }

    def wait(self, eng, tok):
        if tok is None:
            return
        sem, val, owner = tok
        key = id(sem)
        if eng.waited.get(key, -1) >= val:
            return
        eng.e.wait_ge(sem, val)
        eng.waited[key] = val

    def waits(self, eng, toks):
        for t in toks:
            self.wait(eng, t)

    def selfbar(self, eng):
        self.wait(eng, eng.mark())

    def dma(self, queue_eng, kind, out_ap, in_ap, deps=()):
        self.waits(queue_eng, deps)
        pool = self.pools[kind]
        idx = pool["i"] % len(pool["sems"])
        pool["i"] += 1
        sem = pool["sems"][idx]
        prior = pool["uses"][idx]
        if prior:
            self.wait(queue_eng, (sem, 16 * prior, None))
        inst = queue_eng.e.dma_start(out_ap, in_ap)
        inst.then_inc(sem, 16)
        pool["uses"][idx] = prior + 1
        queue_eng.emit(inst)
        return (sem, 16 * (prior + 1), None)


class Guard:

    def __init__(self, prog):
        self.p = prog
        self.marks = {}
        self.readers = {}

    def use(self, eng):
        for tok in list(self.marks.values()):
            self.p.wait(eng, tok)
        for tok in list(self.readers.values()):
            self.p.wait(eng, tok)

    def done(self, eng):
        self.marks[eng.name] = eng.mark()
        self.readers.clear()

    def done_tok(self, name, tok):
        self.marks[name] = tok
        self.readers.clear()

    def use_r(self, eng):
        for tok in list(self.marks.values()):
            self.p.wait(eng, tok)

    def done_r(self, eng):
        self.readers[eng.name] = eng.mark()


def build_program(n_layers=DEPTH, do_top=True, do_moe=True, n_experts=NE,
                  taps=None, stop_after=None):
    taps = taps or []
    nc = bass.Bass("TRN2", target_bir_lowering=False)

    xin = nc.dram_tensor("x", [T, D], F32, kind="ExternalInput").ap()
    Wq = nc.dram_tensor("Wq", [DEPTH, D, D], F32, kind="ExternalInput").ap()
    Wk = nc.dram_tensor("Wk", [DEPTH, D, D], F32, kind="ExternalInput").ap()
    Wv = nc.dram_tensor("Wv", [DEPTH, D, D], F32, kind="ExternalInput").ap()
    Wo = nc.dram_tensor("Wo", [DEPTH, D, D], F32, kind="ExternalInput").ap()
    bo = nc.dram_tensor("bo", [DEPTH, D], F32, kind="ExternalInput").ap()
    W1 = nc.dram_tensor("W1", [DEPTH, D, DFF], F32, kind="ExternalInput").ap()
    W2 = nc.dram_tensor("W2", [DEPTH, DFF, D], F32, kind="ExternalInput").ap()
    b2 = nc.dram_tensor("b2", [DEPTH, D], F32, kind="ExternalInput").ap()
    g1T = nc.dram_tensor("g1T", [DEPTH, 128, 8], F32, kind="ExternalInput").ap()
    b1nT = nc.dram_tensor("b1nT", [DEPTH, 128, 8], F32, kind="ExternalInput").ap()
    g2T = nc.dram_tensor("g2T", [DEPTH, 128, 8], F32, kind="ExternalInput").ap()
    b2nT = nc.dram_tensor("b2nT", [DEPTH, 128, 8], F32, kind="ExternalInput").ap()
    b1T = nc.dram_tensor("b1T", [DEPTH, 128, 32], F32, kind="ExternalInput").ap()
    ctrT = nc.dram_tensor("ctrT", [D, NE], F32, kind="ExternalInput").ap()
    ebb = nc.dram_tensor("ebb", [128, NE], F32, kind="ExternalInput").ap()
    Wsu = nc.dram_tensor("Wsu", [D, FE], F32, kind="ExternalInput").ap()
    Wsd = nc.dram_tensor("Wsd", [FE, D], F32, kind="ExternalInput").ap()
    Wup = nc.dram_tensor("Wup", [NE, D, FE], F32, kind="ExternalInput").ap()
    Wdn = nc.dram_tensor("Wdn", [NE, FE, D], F32, kind="ExternalInput").ap()
    cst = nc.dram_tensor("cst", [128, 672], BF16, kind="ExternalInput").ap()
    out = nc.dram_tensor("out", [T, D], F32, kind="ExternalOutput").ap()
    tap_outs = {}
    for tn, shape, dt in taps:
        tap_outs[tn] = nc.dram_tensor("tap_" + tn, shape, dt,
                                      kind="ExternalOutput").ap()

    with ExitStack() as stack:
        e = stack.enter_context
        x_sb = e(nc.sbuf_tensor("x_sb", [128, NT * 1024], F32))
        aT = e(nc.sbuf_tensor("aT", [128, 8 * T], BF16))
        bigB = e(nc.sbuf_tensor("bigB", [128, 8 * T], BF16))
        wA = e(nc.sbuf_tensor("wA", [128, 18432], BF16))
        qkvA = e(nc.sbuf_tensor("qkvA", [128, 12288], BF16))
        apre = e(nc.sbuf_tensor("apre", [128, 2048], BF16))
        stage = e(nc.sbuf_tensor("stage", [128, 1024], F32))
        smalls = e(nc.sbuf_tensor("smalls", [128, 1536], F32))
        constsB = e(nc.sbuf_tensor("constsB", [128, 672], BF16))
        nbuf = e(nc.sbuf_tensor("nbuf", [128, 256], BF16))
        brow = e(nc.sbuf_tensor("brow", [1, 2048], BF16))
        Pf = e(nc.psum_tensor("Pf", [128, 3072], F32))
        Pb = e(nc.psum_tensor("Pb", [128, 2048], BF16))

        P = Prog(nc, stack)
        te, act, dve, sp, gp = P.te, P.act, P.dve, P.sp, P.gp

        gbank = [Guard(P) for _ in range(8)]
        gstage = Guard(P)
        gapre = Guard(P)
        gaT = Guard(P)
        gslot = {}
        rg = {}

        def RG(name):
            if name not in rg:
                rg[name] = Guard(P)
            return rg[name]

        SREGS = tuple("st%d" % i for i in range(4))
        AREGS = tuple("ap%d" % i for i in range(8))

        def SG(i):
            return RG("st%d" % i)

        def apg(i):
            return RG("ap%d" % i)

        def stage_use_all(eng):
            gstage.use(eng)
            for n in SREGS:
                RG(n).use(eng)

        def apre_use_all(eng):
            gapre.use(eng)
            for n in AREGS:
                RG(n).use(eng)

        def apre_use_half(eng, half):
            gapre.use(eng)
            for i in range(4 * half, 4 * half + 4):
                apg(i).use(eng)

        def apre_done_half(eng, half):
            tok = eng.mark()
            gapre.done_tok(eng.name, tok)
            for i in range(4 * half, 4 * half + 4):
                apg(i).done_tok(eng.name, tok)

        def slot_guard(name):
            if name not in gslot:
                gslot[name] = Guard(P)
            return gslot[name]

        ident = constsB[:, 0:128]
        mask_wk = constsB[:, 128:384]
        mask_w0 = constsB[:, 384:512]
        ones_row = constsB[0:1, 512:640]

        def fbank(i, lo=0, hi=512):
            return Pf[:, i * 512 + lo:i * 512 + hi]

        def fpair(i, n=1024):
            return Pf[:, i * 512:i * 512 + n]

        def bbank(i, lo=0, hi=1024):
            return Pb[:, (i - 6) * 1024 + lo:(i - 6) * 1024 + hi]

        def sm(col, n=1):
            return smalls[:, col:col + n]

        def xt(t, lo=0, hi=1024):
            return x_sb[:, t * 1024 + lo:t * 1024 + hi]

        def aTv(dt, t0, t1):
            return aT[:, dt * T + t0:dt * T + t1]

        def oTv(et, t0, t1):
            return bigB[:, et * T + t0:et * T + t1]

        def wv(off, n):
            return wA[:, off:off + n]

        tok_cst = P.dma(sp, "x", constsB[:], cst)
        xr = xin.rearrange("(a p) d -> p a d", p=128)
        xv = x_sb[:].rearrange("p (a d) -> p a d", a=NT)
        tok_x = P.dma(sp, "x", xv, xr)
        tok_eb = P.dma(sp, "x", smalls[:, EB:EB + 64], ebb)

        gbrow = slot_guard("brow")

        def load_gb(i, deps=()):
            toks = [
                P.dma(sp, "x", smalls[:, GT1:GT1 + 8], g1T[i], deps=deps),
                P.dma(sp, "x", smalls[:, BT1:BT1 + 8], b1nT[i]),
                P.dma(sp, "x", smalls[:, GT2:GT2 + 8], g2T[i]),
                P.dma(sp, "x", smalls[:, BT2:BT2 + 8], b2nT[i]),
                P.dma(sp, "x", smalls[:, B1T:B1T + 32], b1T[i]),
            ]
            gbrow.use(gp)
            t6 = P.dma(gp, "w", brow[0:1, 0:1024], bo[i:i + 1, :])
            t7 = P.dma(gp, "w", brow[0:1, 1024:2048], b2[i:i + 1, :])
            gbrow.done_tok("gp", t7)
            return toks + [t6, t7]

        gb_tok = load_gb(0)
        ct_tok = P.dma(gp, "w",
                       wv(16384, 512).rearrange("p (t n) -> p t n", t=8),
                       ctrT.rearrange("(t p) e -> p t e", p=128))

        P.wait(dve, tok_x)
        for t in range(NT):
            dve.emit(nc.vector.reduce_sum(sm(SUMS + t), xt(t), axis=AX.X))
        sums = [dve.mark()] * NT

        xw_tok = [tok_x] * NT

        def emit_ln(gcol, bcol, sums_toks, inplace_nat=False, gb_toks=()):
            P.waits(dve, sums_toks)
            dve.emit(nc.vector.tensor_scalar(
                sm(NEGMU, NT), sm(SUMS, NT), -1.0 / D, None, op0=ALU.mult))
            negmu_tok = dve.mark()
            P.wait(act, negmu_tok)
            for t in range(NT):
                engv = act if t % 2 == 0 else dve
                if engv is act:
                    stage_use_all(engv)
                    P.wait(engv, xw_tok[t])
                    engv.emit(nc.scalar.activation(
                        stage[:], xt(t), AF.Square, bias=sm(NEGMU + t),
                        scale=1.0, accum_out=sm(VARSS + t)))
                    gstage.done(engv)
                else:
                    gbank[0].use(engv)
                    gbank[1].use(engv)
                    P.wait(engv, xw_tok[t])
                    P.wait(engv, negmu_tok)
                    engv.emit(nc.vector.scalar_tensor_tensor(
                        fpair(0), xt(t), sm(NEGMU + t), xt(t),
                        op0=ALU.add, op1=ALU.mult,
                        accum_out=sm(VARSS + t)))
                    gbank[0].done(engv)
                    gbank[1].done(engv)
            vs_tok = act.mark()
            P.wait(dve, vs_tok)
            P.selfbar(dve)
            dve.emit(nc.vector.tensor_scalar(
                sm(STD, NT), sm(VARSS, NT), 1.0 / D, 1e-5,
                op0=ALU.mult, op1=ALU.add))
            pre_tok = dve.mark()
            P.wait(act, pre_tok)
            act.emit(nc.scalar.activation(sm(STD, NT), sm(STD, NT), AF.Sqrt))
            std_tok = act.mark()
            P.wait(dve, std_tok)
            dve.emit(nc.vector.reciprocal(sm(RECR, NT), sm(STD, NT)))
            P.selfbar(dve)
            dve.emit(nc.vector.tensor_tensor(
                sm(MUR, NT), sm(NEGMU, NT), sm(RECR, NT), op=ALU.mult))
            mur_tok = dve.mark()

            aT_toks = []
            tr_prev = {6: None, 7: None}
            for b in range(NT):
                t = b
                pb = 6 + (b % 2)
                half = b % 2
                ap0 = half * 1024
                neng = act if b % 2 == 0 else dve
                apre_use_half(neng, half)
                P.wait(neng, mur_tok)
                P.wait(neng, xw_tok[t])
                if neng is act:
                    neng.emit(nc.scalar.activation(
                        apre[:, ap0:ap0 + 1024], xt(t), AF.Identity,
                        bias=sm(MUR + t), scale=sm(RECR + t)))
                else:
                    neng.emit(nc.vector.tensor_scalar(
                        apre[:, ap0:ap0 + 1024], xt(t),
                        sm(NEGMU + t), sm(RECR + t),
                        op0=ALU.add, op1=ALU.mult))
                if inplace_nat:
                    P.selfbar(neng)
                    if neng is act:
                        neng.emit(nc.scalar.activation(
                            xt(t), xt(t), AF.Identity,
                            bias=sm(MUR + t), scale=sm(RECR + t)))
                    else:
                        neng.emit(nc.vector.tensor_scalar(
                            xt(t), xt(t), sm(NEGMU + t), sm(RECR + t),
                            op0=ALU.add, op1=ALU.mult))
                    xw_tok[t] = neng.mark()
                apre_done_half(neng, half)
                norm_tok = neng.mark()
                g6 = gbank[6]
                g7 = gbank[7]
                g6.use(te)
                g7.use(te)
                apre_use_half(te, half)
                P.wait(te, norm_tok)
                P.wait(te, tok_cst)
                for dt in range(8):
                    pbk = 6 + (dt >= 4)
                    te.emit(nc.tensor.transpose(
                        bbank(pbk, (dt % 4) * 128, (dt % 4 + 1) * 128),
                        apre[:, ap0 + dt * 128:ap0 + (dt + 1) * 128],
                        ident))
                g6.done(te)
                g7.done(te)
                apre_done_half(te, half)
                tr_tok = te.mark()
                for dt in range(8):
                    eng = act if dt < 4 else dve
                    g = g6 if dt < 4 else g7
                    g.use(eng)
                    gaT.use(eng)
                    P.wait(eng, tr_tok)
                    P.waits(eng, gb_toks)
                    src = bbank(6 + (dt >= 4), (dt % 4) * 128,
                                (dt % 4) * 128 + 128)
                    dst = aTv(dt, t * 128, (t + 1) * 128)
                    if gcol is None:
                        if eng is act:
                            eng.emit(nc.scalar.copy(dst, src))
                        else:
                            eng.emit(nc.vector.tensor_copy(dst, src))
                    else:
                        if eng is act:
                            eng.emit(nc.scalar.activation(
                                dst, src, AF.Identity,
                                bias=sm(bcol + dt), scale=sm(gcol + dt)))
                        else:
                            eng.emit(nc.vector.tensor_scalar(
                                dst, src, sm(gcol + dt), sm(bcol + dt),
                                op0=ALU.mult, op1=ALU.add))
                    g.done(eng)
                aT_toks.append(act.mark())
                aT_toks.append(dve.mark())
            return aT_toks

        WPAIR = {0: 14336, 1: 2048}

        def load_wpair(W, i, h0, slot=0):
            g = slot_guard("wpair%d" % slot)
            g.use(gp)
            src = W[i][:, h0 * 128:(h0 + 2) * 128].rearrange(
                "(t p) n -> p t n", p=128)
            dst = wv(WPAIR[slot], 2048).rearrange("p (t n) -> p t n", t=8)
            tok = P.dma(gp, "w", dst, src)
            g.done_tok("gp", tok)
            return (tok, slot)

        def load_whead(W, i, h, parity, which):
            g = slot_guard("qkv%d_%d" % (parity, which))
            g.use(gp)
            src = W[i][:, h * 128:(h + 1) * 128].rearrange(
                "(t p) n -> p t n", p=128)
            dst = wv(parity * 3072 + which * 1024, 1024).rearrange(
                "p (t n) -> p t n", t=8)
            tok = P.dma(gp, "w", dst, src)
            g.done_tok("gp", tok)
            return tok

        def load_wo(i):
            g = slot_guard("wo")
            g.use(gp)
            src = Wo[i].rearrange("(t p) n -> p t n", p=128)
            dst = wv(6144, 8192).rearrange("p (t n) -> p t n", t=8)
            tok = P.dma(gp, "w", dst, src)
            g.done_tok("gp", tok)
            return tok

        def load_up(src2d, parity):
            g = slot_guard("w1_%d" % parity)
            g.use(gp)
            src = src2d.rearrange("(t p) n -> p t n", p=128)
            dst = wv(parity * 4096, 4096).rearrange("p (t n) -> p t n", t=8)
            tok = P.dma(gp, "w", dst, src)
            g.done_tok("gp", tok)
            return tok

        def load_dn(src2d, parity, ttiles):
            g = slot_guard("w2_%d" % parity)
            g.use(gp)
            src = src2d.rearrange("(t p) n -> p t n", p=128)
            dst = wv(8192 + parity * 4096, 4096).rearrange(
                "p (t n) -> p t n", t=ttiles)
            tok = P.dma(gp, "w", dst, src)
            g.done_tok("gp", tok)
            return tok

        def emit_attention(i, aT_toks):
            def v2v(t, c0, c1):
                return qkvA[:, t * 256 + c0:t * 256 + c1]

            def k2v(t, c0, c1):
                return qkvA[:, 4096 + t * 256 + c0:4096 + t * 256 + c1]

            def kTv(p, t0, t1):
                return qkvA[:, 4096 + p * 2048 + t0:4096 + p * 2048 + t1]

            def qTv(p, t0, t1):
                return qkvA[:, 8192 + p * 2048 + t0:8192 + p * 2048 + t1]

            def drive(*gens):
                live = list(gens)
                while live:
                    for gen in list(live):
                        try:
                            next(gen)
                        except StopIteration:
                            live.remove(gen)

            pbctr = [0]

            def v2_gen(wtok_slot, v_prog):
                wtok, slot = wtok_slot
                base = WPAIR[slot]
                for t in range(NT):
                    pb = pbctr[0] % 2
                    pbctr[0] += 1
                    g = gbank[pb]
                    g.use(te)
                    if t == 0:
                        P.waits(te, aT_toks)
                        P.wait(te, wtok)
                    for dt in range(8):
                        te.emit(nc.tensor.matmul(
                            fbank(pb, 0, 256),
                            aTv(dt, t * 128, (t + 1) * 128),
                            wv(base + dt * 256, 256),
                            start=(dt == 0), stop=(dt == 7)))
                    g.done(te)
                    mm_tok = te.mark()
                    eng = act if t % 2 == 0 else dve
                    g.use(eng)
                    P.wait(eng, mm_tok)
                    if eng is act:
                        eng.emit(nc.scalar.copy(v2v(t, 0, 256),
                                                fbank(pb, 0, 256)))
                    else:
                        eng.emit(nc.vector.tensor_copy(v2v(t, 0, 256),
                                                       fbank(pb, 0, 256)))
                    g.done(eng)
                    v_prog.append(eng.mark())
                    yield
                slot_guard("wpair%d" % slot).done(te)
                gaT.done(te)
                v_prog.append(v_prog[-1])

            def pT_gen(wbase, dst_view, scale, wtok, wslot, prog):
                for c in range(4):
                    pb = pbctr[0] % 2
                    pbctr[0] += 1
                    g = gbank[pb]
                    g.use(te)
                    if c == 0:
                        P.waits(te, aT_toks)
                        P.wait(te, wtok)
                    for dt in range(8):
                        te.emit(nc.tensor.matmul(
                            fbank(pb),
                            wv(wbase + dt * 128, 128),
                            aTv(dt, c * 512, (c + 1) * 512),
                            start=(dt == 0), stop=(dt == 7)))
                    g.done(te)
                    mm_tok = te.mark()
                    eng = act if (c % 2 == 0 and scale == 1.0) else dve
                    g.use(eng)
                    P.wait(eng, mm_tok)
                    dst = dst_view(c * 512, (c + 1) * 512)
                    if scale != 1.0:
                        eng.emit(nc.vector.tensor_scalar(
                            dst, fbank(pb), scale, None, op0=ALU.mult))
                    elif eng is act:
                        eng.emit(nc.scalar.copy(dst, fbank(pb)))
                    else:
                        eng.emit(nc.vector.tensor_copy(dst, fbank(pb)))
                    g.done(eng)
                    prog.append(eng.mark())
                    yield
                wslot.done(te)
                gaT.done(te)

            def win_gen(h, v_prog, q_prog, k_prog):
                par = h % 2
                hh = par
                tpb = 6 + par
                if par == 0:
                    uslot = (0, 1)
                    nslot = 2
                    sslot = (0, 2)
                    atn, at_off, atreg = nbuf, 0, None
                else:
                    uslot = (4, 6)
                    nslot = 7
                    sslot = (1, 3)
                    atn, at_off, atreg = apre, 1280, apg(5)
                nreg = apg(nslot)
                nbase = nslot * 256
                ws_used = [None, None]
                state = None

                def front(w):
                    njj = 1 if w == 0 else 2
                    jt0 = max(w - 1, 0)
                    nj = njj * 128
                    dpb = 2 + par + 2 * (w % 2)
                    g = gbank[dpb]
                    g.use(te)
                    P.wait(te, q_prog[w // 4])
                    P.wait(te, k_prog[w // 4])
                    te.emit(nc.tensor.matmul(
                        fbank(dpb, 0, nj),
                        qTv(par, w * 128, (w + 1) * 128),
                        kTv(par, jt0 * 128, (w + 1) * 128),
                        start=True, stop=True))
                    g.done(te)
                    return te.mark(), dpb, njj, jt0, nj

                def front2(w, dots_tok, dpb, nj):
                    ws = WS0 + 3 * (par * 2 + w % 2)
                    ss = sslot[w % 2]
                    sreg = SG(ss)
                    sbase = ss * 256
                    us = uslot[w % 2]
                    ureg = apg(us)
                    ubase = us * 256
                    mask = mask_w0 if w == 0 else mask_wk
                    g = gbank[dpb]
                    g.use(dve)
                    gstage.use(dve)
                    sreg.use(dve)
                    P.wait(dve, dots_tok)
                    P.wait(dve, tok_cst)
                    P.wait(dve, ws_used[w % 2])
                    dve.emit(nc.vector.tensor_tensor(
                        stage[:, sbase:sbase + nj], fbank(dpb, 0, nj),
                        mask[:, 0:nj], op=ALU.add))
                    g.done(dve)
                    P.selfbar(dve)
                    dve.emit(nc.vector.reduce_max(
                        sm(ws), stage[:, sbase:sbase + nj], axis=AX.X,
                        negate=True))
                    sreg.done(dve)
                    nm_tok = dve.mark()
                    gapre.use(act)
                    sreg.use(act)
                    ureg.use(act)
                    P.wait(act, nm_tok)
                    act.emit(nc.scalar.activation(
                        apre[:, ubase:ubase + nj], stage[:, sbase:sbase + nj],
                        AF.Exp, bias=sm(ws), scale=1.0,
                        accum_out=sm(ws + 1)))
                    sreg.done(act)
                    ureg.done(act)
                    return act.mark(), ws, ureg, ubase

                def tail1(st):
                    (w, exp_tok, ws, ureg, ubase, njj, jt0, nj) = st
                    gapre.use(dve)
                    ureg.use(dve)
                    nreg.use(dve)
                    P.wait(dve, exp_tok)
                    dve.emit(nc.vector.reciprocal(sm(ws + 2), sm(ws + 1)))
                    P.selfbar(dve)
                    dve.emit(nc.vector.tensor_scalar(
                        apre[:, nbase:nbase + nj], apre[:, ubase:ubase + nj],
                        sm(ws + 2), None, op0=ALU.mult))
                    ureg.done(dve)
                    nreg.done(dve)
                    ws_used[w % 2] = dve.mark()
                    return ws_used[w % 2]

                def tail2(st, an_tok):
                    (w, exp_tok, ws, ureg, ubase, njj, jt0, nj) = st
                    gt = gbank[tpb]
                    gt.use(te)
                    gapre.use(te)
                    nreg.use(te)
                    P.wait(te, an_tok)
                    for jj in range(njj):
                        te.emit(nc.tensor.transpose(
                            bbank(tpb, jj * 128, (jj + 1) * 128),
                            apre[:, nbase + jj * 128:nbase + (jj + 1) * 128],
                            ident))
                    gt.done(te)
                    nreg.done(te)
                    trt_tok = te.mark()
                    eeng = dve if par == 0 else act
                    gt.use(eeng)
                    if atreg is not None:
                        atreg.use(eeng)
                    P.wait(eeng, trt_tok)
                    if eeng is dve:
                        eeng.emit(nc.vector.tensor_copy(
                            atn[:, at_off:at_off + nj], bbank(tpb, 0, nj)))
                    else:
                        eeng.emit(nc.scalar.copy(
                            atn[:, at_off:at_off + nj], bbank(tpb, 0, nj)))
                    gt.done(eeng)
                    if atreg is not None:
                        atreg.done(eeng)
                    return eeng.mark()

                def tail3(st, at_tok):
                    (w, exp_tok, ws, ureg, ubase, njj, jt0, nj) = st
                    obk = 2 + par + 2 * ((w + 1) % 2)
                    go = gbank[obk]
                    go.use(te)
                    if atreg is not None:
                        atreg.use(te)
                    P.wait(te, at_tok)
                    for jj in range(njj):
                        jw = jt0 + jj
                        P.wait(te, v_prog[jw])
                        te.emit(nc.tensor.matmul(
                            fbank(obk, 256, 384),
                            v2v(jw, hh * 128, (hh + 1) * 128),
                            atn[:, at_off + jj * 128:at_off + (jj + 1) * 128],
                            start=(jj == 0), stop=(jj == njj - 1)))
                    go.done(te)
                    if atreg is not None:
                        atreg.done(te)
                    o_tok = te.mark()
                    go.use(act)
                    P.wait(act, o_tok)
                    act.emit(nc.scalar.copy(
                        oTv(h, w * 128, (w + 1) * 128), fbank(obk, 256, 384)))
                    go.done(act)

                for w in range(NW):
                    while len(q_prog) <= w // 4 or len(k_prog) <= w // 4 or \
                            len(v_prog) <= w:
                        yield
                    dots_tok, dpb, njj, jt0, nj = front(w)
                    yield
                    exp_tok, ws, ureg, ubase = front2(w, dots_tok, dpb, nj)
                    yield
                    if state is not None:
                        an = tail1(state)
                        yield
                        at = tail2(state, an)
                        yield
                        tail3(state, at)
                        yield
                    state = (w, exp_tok, ws, ureg, ubase, njj, jt0, nj)
                an = tail1(state)
                yield
                at = tail2(state, an)
                yield
                tail3(state, at)

            wo_tok = None
            vp0 = load_wpair(Wv, i, 0, slot=0)
            wq0 = load_whead(Wq, i, 0, 0, 0)
            wk0 = load_whead(Wk, i, 0, 0, 1)
            wq1 = load_whead(Wq, i, 1, 1, 0)
            wk1 = load_whead(Wk, i, 1, 1, 1)
            wo_tok = load_wo(i)
            vpr, q0p, k0p, q1p, k1p = [], [], [], [], []
            gens = [
                v2_gen(vp0, vpr),
                pT_gen(0, lambda a, b: qTv(0, a, b), float(DH) ** -0.5,
                       wq0, slot_guard("qkv0_0"), q0p),
                pT_gen(1024, lambda a, b: kTv(0, a, b), 1.0,
                       wk0, slot_guard("qkv0_1"), k0p),
                pT_gen(3072, lambda a, b: qTv(1, a, b), float(DH) ** -0.5,
                       wq1, slot_guard("qkv1_0"), q1p),
                pT_gen(3072 + 1024, lambda a, b: kTv(1, a, b), 1.0,
                       wk1, slot_guard("qkv1_1"), k1p),
                win_gen(0, vpr, q0p, k0p),
                win_gen(1, vpr, q1p, k1p),
            ]
            drive(*gens)
            vp2 = load_wpair(Wv, i, 2, slot=0)
            wq2 = load_whead(Wq, i, 2, 0, 0)
            wk2 = load_whead(Wk, i, 2, 0, 1)
            wq3 = load_whead(Wq, i, 3, 1, 0)
            wk3 = load_whead(Wk, i, 3, 1, 1)
            vpr, q0p, k0p, q1p, k1p = [], [], [], [], []
            gens = [
                v2_gen(vp2, vpr),
                pT_gen(0, lambda a, b: qTv(0, a, b), float(DH) ** -0.5,
                       wq2, slot_guard("qkv0_0"), q0p),
                pT_gen(1024, lambda a, b: kTv(0, a, b), 1.0,
                       wk2, slot_guard("qkv0_1"), k0p),
                pT_gen(3072, lambda a, b: qTv(1, a, b), float(DH) ** -0.5,
                       wq3, slot_guard("qkv1_0"), q1p),
                pT_gen(3072 + 1024, lambda a, b: kTv(1, a, b), 1.0,
                       wk3, slot_guard("qkv1_1"), k1p),
                win_gen(2, vpr, q0p, k0p),
                win_gen(3, vpr, q1p, k1p),
            ]
            drive(*gens)
            lin_start = te.mark()
            slot_guard("wpair1").done_tok("te", lin_start)
            vpl = load_wpair(Wv, i, NLOC, slot=0)
            kpl = load_wpair(Wk, i, NLOC, slot=1)

            def k2_gen(kp_tok_pair, k_prog):
                kp_tok_v, kslot = kp_tok_pair
                kbase = WPAIR[kslot]
                gk0 = slot_guard("qkv0_1")
                gk1 = slot_guard("qkv1_1")
                e0 = apg(0)
                e1 = apg(1)
                for t in range(NT):
                    pb = 2
                    g = gbank[pb]
                    g.use(te)
                    if t == 0:
                        P.waits(te, aT_toks)
                        P.wait(te, kp_tok_v)
                        gk0.use(te)
                        gk1.use(te)
                    for dt in range(8):
                        te.emit(nc.tensor.matmul(
                            fbank(pb, 0, 256),
                            aTv(dt, t * 128, (t + 1) * 128),
                            wv(kbase + dt * 256, 256),
                            start=(dt == 0), stop=(dt == 7)))
                    g.done(te)
                    mm_tok = te.mark()
                    yield
                    g.use(dve)
                    e0.use(dve)
                    P.wait(dve, mm_tok)
                    dve.emit(nc.vector.tensor_scalar(
                        apre[:, 0:256], fbank(pb, 0, 256), 0.0, None,
                        op0=ALU.min))
                    e0.done(dve)
                    mn_tok = dve.mark()
                    e0.use(act)
                    e1.use(act)
                    P.wait(act, mn_tok)
                    act.emit(nc.scalar.activation(
                        apre[:, 256:512], apre[:, 0:256], AF.Exp))
                    e0.done(act)
                    e1.done(act)
                    ex_tok = act.mark()
                    g.use(dve)
                    e1.use(dve)
                    P.wait(dve, ex_tok)
                    dve.emit(nc.vector.scalar_tensor_tensor(
                        k2v(t, 0, 256), fbank(pb, 0, 256), 0.0,
                        apre[:, 256:512], op0=ALU.max, op1=ALU.add))
                    g.done(dve)
                    e1.done(dve)
                    k_prog.append(dve.mark())
                    yield
                gk0.done_tok("dve", k_prog[-1])
                gk1.done_tok("dve", k_prog[-1])
                slot_guard("wpair%d" % kslot).done(te)
                gaT.done(te)

            def qgen(hh, qp_list, q_prog):
                while not qp_list:
                    yield
                qp_tok, qslot_i = qp_list[0]
                qbase = WPAIR[qslot_i]
                par = hh
                pb = hh
                tpb = 6 + hh
                sreg = SG(hh)
                qreg = apg(4)
                sbase = hh * 256
                ap_off = 1024 + hh * 128
                gq = slot_guard("qkv%d_0" % par)
                g = gbank[pb]
                gt = gbank[tpb]
                qa_last = None
                for t in range(NT):
                    g.use(te)
                    if t == 0:
                        P.wait(te, qp_tok)
                        gq.use(te)
                    for dt in range(8):
                        te.emit(nc.tensor.matmul(
                            fbank(pb, 0, 128),
                            aTv(dt, t * 128, (t + 1) * 128),
                            wv(qbase + dt * 256 + hh * 128, 128),
                            start=(dt == 0), stop=(dt == 7)))
                    g.done(te)
                    mm_tok = te.mark()
                    yield
                    qs = QS0 + 3 * (hh * 2 + t % 2)
                    g.use(dve)
                    P.wait(dve, mm_tok)
                    dve.emit(nc.vector.reduce_max(
                        sm(qs), fbank(pb, 0, 128), axis=AX.X, negate=True))
                    nm_tok = dve.mark()
                    g.use(act)
                    gstage.use(act)
                    sreg.use(act)
                    P.wait(act, nm_tok)
                    act.emit(nc.scalar.activation(
                        stage[:, sbase:sbase + 128], fbank(pb, 0, 128),
                        AF.Exp, bias=sm(qs), scale=1.0, accum_out=sm(qs + 1)))
                    g.done(act)
                    sreg.done(act)
                    ex_tok = act.mark()
                    yield
                    P.wait(dve, ex_tok)
                    dve.emit(nc.vector.reciprocal(sm(qs + 2), sm(qs + 1)))
                    rc_tok = dve.mark()
                    gapre.use(act)
                    qreg.use(act)
                    sreg.use(act)
                    P.wait(act, rc_tok)
                    P.wait(act, qa_last)
                    act.emit(nc.scalar.activation(
                        apre[:, ap_off:ap_off + 128],
                        stage[:, sbase:sbase + 128],
                        AF.Copy, scale=sm(qs + 2)))
                    sreg.done(act)
                    qreg.done(act)
                    qn_tok = act.mark()
                    gt.use(te)
                    qreg.use(te)
                    P.wait(te, qn_tok)
                    te.emit(nc.tensor.transpose(
                        bbank(tpb, 0, 128),
                        apre[:, ap_off:ap_off + 128], ident))
                    gt.done(te)
                    qreg.done(te)
                    qa_last = te.mark()
                    gt.use(dve)
                    P.wait(dve, qa_last)
                    dve.emit(nc.vector.tensor_copy(
                        qTv(par, t * 128, (t + 1) * 128),
                        bbank(tpb, 0, 128)))
                    gt.done(dve)
                    q_prog.append(dve.mark())
                    yield
                gq.done_tok("dve", dve.mark())
                slot_guard("wpair%d" % qslot_i).done(te)
                gaT.done(te)

            def buck_gen(h, hh, v_prog, k_prog, q_prog):
                sbk = 4 + hh
                obk = 3
                oof = hh * 64
                gS = gbank[sbk]
                gO = gbank[obk]
                er = SG(2) if hh == 0 else SG(3)
                nreg = apg(6) if hh == 0 else apg(7)
                s0 = 512 + hh * 256
                nb_base = 1536 + hh * 256
                dve.emit(nc.vector.memset(oTv(h, 0, 64), 0.0))
                gstage.use(dve)
                er.use(dve)
                dve.emit(nc.vector.memset(stage[:, s0:s0 + 129], 0.0))
                er.done(dve)
                n_tok = None
                lr = LRC + 2 * hh
                for u in range(NB):
                    tt = u // 2
                    r0 = (u % 2) * 64
                    if u < NB - 1:
                        while len(k_prog) <= tt or len(v_prog) <= tt:
                            yield
                        gS.use(te)
                        P.wait(te, k_prog[tt])
                        P.wait(te, v_prog[tt])
                        klhs = k2v(tt, hh * 128,
                                   (hh + 1) * 128)[r0:r0 + 64, :]
                        te.emit(nc.tensor.matmul(
                            fbank(sbk, 0, 128), klhs,
                            v2v(tt, hh * 128, (hh + 1) * 128)[r0:r0 + 64, :],
                            start=True, stop=False))
                        te.emit(nc.tensor.matmul(
                            fbank(sbk, 128, 129), klhs,
                            constsB[r0:r0 + 64, 640:641],
                            start=False, stop=True))
                        gS.done(te)
                        s_tok = te.mark()
                    if u > 0:
                        while len(q_prog) <= u // 2:
                            yield
                        gO.use(te)
                        nreg.use(te)
                        P.wait(te, n_tok)
                        P.wait(te, q_prog[u // 2])
                        te.emit(nc.tensor.matmul(
                            fbank(obk, oof, oof + 64),
                            apre[:, nb_base + ((u - 1) % 2) * 128:
                                 nb_base + ((u - 1) % 2 + 1) * 128],
                            qTv(hh, u * 64, (u + 1) * 64),
                            start=True, stop=True))
                        gO.done(te)
                        nreg.done(te)
                        ou_tok = te.mark()
                        gO.use(act)
                        P.wait(act, ou_tok)
                        act.emit(nc.scalar.copy(
                            oTv(h, u * 64, (u + 1) * 64),
                            fbank(obk, oof, oof + 64)))
                        gO.done(act)
                    if u == NB - 1:
                        break
                    yield
                    gS.use(dve)
                    gstage.use(dve)
                    er.use(dve)
                    P.wait(dve, s_tok)
                    dve.emit(nc.vector.tensor_tensor(
                        stage[:, s0:s0 + 129], fbank(sbk, 0, 129),
                        stage[:, s0:s0 + 129], op=ALU.add))
                    P.selfbar(dve)
                    dve.emit(nc.vector.tensor_scalar(
                        sm(lr), stage[:, s0 + 128:s0 + 129], 1e-6, None,
                        op0=ALU.add))
                    P.selfbar(dve)
                    dve.emit(nc.vector.reciprocal(sm(lr + 1), sm(lr)))
                    gS.done(dve)
                    er.done(dve)
                    rc_tok = dve.mark()
                    gstage.use(act)
                    er.use(act)
                    nreg.use(act)
                    P.wait(act, rc_tok)
                    act.emit(nc.scalar.activation(
                        apre[:, nb_base + (u % 2) * 128:
                             nb_base + (u % 2 + 1) * 128],
                        stage[:, s0:s0 + 128], AF.Copy, scale=sm(lr + 1)))
                    er.done(act)
                    nreg.done(act)
                    n_tok = act.mark()
                    yield

            def q_loader(v_prog, h0, slot, qp_list):
                while len(v_prog) < NT + 1:
                    yield
                qp_list.append(load_wpair(Wq, i, h0, slot=slot))

            vpr, kpr, qp0, qp1 = [], [], [], []
            qpl = []
            gens = [
                v2_gen(vpl, vpr),
                q_loader(vpr, NLOC, 0, qpl),
                k2_gen(kpl, kpr),
                qgen(0, qpl, qp0), qgen(1, qpl, qp1),
                buck_gen(NLOC, 0, vpr, kpr, qp0),
                buck_gen(NLOC + 1, 1, vpr, kpr, qp1),
            ]
            drive(*gens)
            vpl1 = load_wpair(Wv, i, NLOC + 2, slot=1)
            kpl1 = load_wpair(Wk, i, NLOC + 2, slot=0)
            vpr, kpr, qp0, qp1 = [], [], [], []
            qpl1 = []
            gens = [
                v2_gen(vpl1, vpr),
                q_loader(vpr, NLOC + 2, 1, qpl1),
                k2_gen(kpl1, kpr),
                qgen(0, qpl1, qp0), qgen(1, qpl1, qp1),
                buck_gen(NLOC + 2, 0, vpr, kpr, qp0),
                buck_gen(NLOC + 3, 1, vpr, kpr, qp1),
            ]
            drive(*gens)
            oT_toks = [act.mark(), dve.mark()]

            gwo = slot_guard("wo")
            res_toks = []
            for t in range(NT):
                pair = (t % 2) * 2
                g0, g1 = gbank[pair], gbank[pair + 1]
                g0.use(te)
                g1.use(te)
                if t == 0:
                    P.waits(te, oT_toks)
                    P.wait(te, wo_tok)
                    P.wait(te, gb_tok[5])
                    gwo.use(te)
                    gbrow.use(te)
                for half in range(2):
                    pbk = pair + half
                    for et in range(8):
                        te.emit(nc.tensor.matmul(
                            fbank(pbk),
                            oTv(et, t * 128, (t + 1) * 128),
                            wv(6144 + et * 1024 + half * 512, 512),
                            start=(et == 0), stop=False))
                    te.emit(nc.tensor.matmul(
                        fbank(pbk), ones_row,
                        brow[0:1, half * 512:(half + 1) * 512],
                        start=False, stop=True))
                g0.done(te)
                g1.done(te)
                mm_tok = te.mark()
                g0.use(dve)
                g1.use(dve)
                P.wait(dve, mm_tok)
                P.wait(dve, xw_tok[t])
                dve.emit(nc.vector.scalar_tensor_tensor(
                    xt(t), fpair(pair), 1.0, xt(t),
                    op0=ALU.mult, op1=ALU.add, accum_out=sm(SUMS + t)))
                g0.done(dve)
                g1.done(dve)
                xw_tok[t] = dve.mark()
                res_toks.append(xw_tok[t])
            gwo.done(te)
            gbrow.done(te)
            return res_toks

        def emit_ffn(i, aT_toks):
            res_toks = [None] * NT
            wtoks = {0: (load_up(W1[i][:, 0:512], 0),
                         load_dn(W2[i][0:512, :], 0, 4))}
            for c in range(8):
                par = c % 2
                if c + 1 < 8:
                    wtoks[c + 1] = (
                        load_up(W1[i][:, (c + 1) * 512:(c + 2) * 512], 1 - par),
                        load_dn(W2[i][(c + 1) * 512:(c + 2) * 512, :],
                                1 - par, 4))
                w1t, w2t = wtoks.pop(c)
                gw1 = slot_guard("w1_%d" % par)
                gw2 = slot_guard("w2_%d" % par)
                fft_toks = []
                for fs in range(4):
                    for tc in range(4):
                        pb = (fs * 4 + tc) % 2
                        g = gbank[pb]
                        g.use(te)
                        if fs == 0 and tc == 0:
                            P.waits(te, aT_toks)
                            P.wait(te, w1t)
                            gw1.use(te)
                        for dt in range(8):
                            te.emit(nc.tensor.matmul(
                                fbank(pb),
                                wv(par * 4096 + dt * 512 + fs * 128, 128),
                                aTv(dt, tc * 512, (tc + 1) * 512),
                                start=(dt == 0), stop=(dt == 7)))
                        g.done(te)
                        mm_tok = te.mark()
                        g.use(act)
                        P.wait(act, mm_tok)
                        if fs == 0 and tc == 0:
                            P.wait(act, gb_tok[4])
                        act.emit(nc.scalar.activation(
                            bigB[:, par * 8192 + fs * 2048 + tc * 512:
                                 par * 8192 + fs * 2048 + (tc + 1) * 512],
                            fbank(pb), AF.Gelu,
                            bias=sm(B1T + c * 4 + fs), scale=1.0))
                        g.done(act)
                        fft_toks.append(act.mark())
                gw1.done(te)
                gaT.done(te)
                for t in range(NT):
                    pair = 2 + (t % 2) * 2
                    g0, g1 = gbank[pair], gbank[pair + 1]
                    g0.use(te)
                    g1.use(te)
                    if t == 0:
                        P.waits(te, fft_toks)
                        P.wait(te, w2t)
                        gw2.use(te)
                        if c == 7:
                            P.wait(te, gb_tok[6])
                            gbrow.use(te)
                    for half in range(2):
                        pbk = pair + half
                        for fs in range(4):
                            te.emit(nc.tensor.matmul(
                                fbank(pbk),
                                bigB[:, par * 8192 + fs * 2048 + t * 128:
                                     par * 8192 + fs * 2048 + (t + 1) * 128],
                                wv(8192 + par * 4096 + fs * 1024 + half * 512,
                                   512),
                                start=(fs == 0),
                                stop=(fs == 3 and c != 7)))
                        if c == 7:
                            te.emit(nc.tensor.matmul(
                                fbank(pbk), ones_row,
                                brow[0:1, 1024 + half * 512:
                                     1024 + (half + 1) * 512],
                                start=False, stop=True))
                    g0.done(te)
                    g1.done(te)
                    mm_tok = te.mark()
                    g0.use(dve)
                    g1.use(dve)
                    P.wait(dve, mm_tok)
                    P.wait(dve, xw_tok[t])
                    if c == 7:
                        dve.emit(nc.vector.scalar_tensor_tensor(
                            xt(t), fpair(pair), 1.0, xt(t),
                            op0=ALU.mult, op1=ALU.add,
                            accum_out=sm(SUMS + t)))
                    else:
                        dve.emit(nc.vector.tensor_tensor(
                            xt(t), fpair(pair), xt(t), op=ALU.add))
                    g0.done(dve)
                    g1.done(dve)
                    xw_tok[t] = dve.mark()
                    if c == 7:
                        res_toks[t] = xw_tok[t]
                gw2.done(te)
                if c == 7:
                    gbrow.done(te)
            return res_toks

        stopped = False
        for i in range(n_layers):
            P.wait(gp, te.mark())
            aT_toks = emit_ln(GT1, BT1, sums, gb_toks=gb_tok[0:2])
            if stop_after == "ln1" and i == n_layers - 1:
                stopped = True
                break
            sums = emit_attention(i, aT_toks)
            if stop_after == "attn" and i == n_layers - 1:
                stopped = True
                break
            P.wait(gp, te.mark())
            aT_toks = emit_ln(GT2, BT2, sums, gb_toks=gb_tok[2:4])
            if stop_after == "ln2" and i == n_layers - 1:
                stopped = True
                break
            sums = emit_ffn(i, aT_toks)
            if i + 1 < n_layers:
                gb_tok = load_gb(i + 1, deps=[act.mark(), dve.mark()])
        if stopped:
            do_top = False

        if do_top:
            if do_moe:
                def eload(eix, parity):
                    if eix == -1:
                        return (load_up(Wsu, parity), load_dn(Wsd, parity, 4))
                    return (load_up(Wup[eix], parity),
                            load_dn(Wdn[eix], parity, 4))

                ex_list = [-1] + list(range(n_experts))
                P.wait(gp, te.mark())
                etoks = {-1: eload(-1, 0)}
                if len(ex_list) > 1:
                    etoks[0] = eload(0, 1)
            for t in range(NT):
                stage_use_all(sp)
                x0t = P.dma(sp, "x", stage[:], xin[t * 128:(t + 1) * 128, :])
                gstage.done_tok("sp", x0t)
                stage_use_all(dve)
                P.wait(dve, xw_tok[t])
                dve.emit(nc.vector.scalar_tensor_tensor(
                    xt(t), stage[:], 1.0, xt(t), op0=ALU.mult, op1=ALU.add,
                    accum_out=sm(SUMS + t)))
                gstage.done(dve)
                xw_tok[t] = dve.mark()
            sums = [xw_tok[t] for t in range(NT)]
            aT_toks = emit_ln(None, None, sums, inplace_nat=True)

            if do_moe:
                cw_toks = []
                for t in range(NT):
                    pb = 2 + t % 2
                    g = gbank[pb]
                    g.use(te)
                    if t == 0:
                        P.waits(te, aT_toks)
                        P.wait(te, ct_tok)
                    for dt in range(8):
                        te.emit(nc.tensor.matmul(
                            fbank(pb, 0, 64),
                            aTv(dt, t * 128, (t + 1) * 128),
                            wv(16384 + dt * 64, 64),
                            start=(dt == 0), stop=(dt == 7)))
                    g.done(te)
                    aff_tok = te.mark()
                    c0 = TMP
                    g.use(dve)
                    P.wait(dve, aff_tok)
                    if t == 0:
                        P.wait(dve, tok_eb)
                    dve.emit(nc.vector.tensor_tensor(
                        sm(c0, 64), fbank(pb, 0, 64), smalls[:, EB:EB + 64],
                        op=ALU.add))
                    P.selfbar(dve)
                    dve.emit(nc.vector.max(sm(c0 + 64, 8), sm(c0, 64)))
                    P.selfbar(dve)
                    dve.emit(nc.vector.tensor_scalar(
                        sm(c0 + 72, 64), sm(c0, 64), sm(c0 + 71), None,
                        op0=ALU.is_ge))
                    b8_tok = dve.mark()
                    g.use(act)
                    P.wait(act, b8_tok)
                    act.emit(nc.scalar.activation(
                        sm(c0 + 140, 64), fbank(pb, 0, 64), AF.Sigmoid))
                    g.done(act)
                    sg_tok = act.mark()
                    g.done(dve)
                    P.wait(dve, sg_tok)
                    dve.emit(nc.vector.scalar_tensor_tensor(
                        sm(c0 + 140, 64), sm(c0 + 72, 64), 1.0,
                        sm(c0 + 140, 64), op0=ALU.mult, op1=ALU.mult,
                        accum_out=sm(c0 + 137)))
                    P.selfbar(dve)
                    dve.emit(nc.vector.tensor_scalar(
                        sm(c0 + 138), sm(c0 + 137), 1e-8, None, op0=ALU.add))
                    P.selfbar(dve)
                    dve.emit(nc.vector.reciprocal(sm(c0 + 139), sm(c0 + 138)))
                    P.selfbar(dve)
                    dve.emit(nc.vector.tensor_scalar(
                        sm(CW + t * 64, 64), sm(c0 + 140, 64), sm(c0 + 139),
                        None, op0=ALU.mult))
                    cw_toks.append(dve.mark())

                for idx, eix in enumerate(ex_list):
                    par = idx % 2
                    w_up_tok, w_dn_tok = etoks.pop(eix)
                    nidx = idx + 1
                    if nidx < len(ex_list) and ex_list[nidx] not in etoks:
                        etoks[ex_list[nidx]] = eload(ex_list[nidx], 1 - par)
                    gw1 = slot_guard("w1_%d" % par)
                    gw2 = slot_guard("w2_%d" % par)
                    hT_toks = []
                    for fs in range(4):
                        for tc in range(4):
                            pb = (fs * 4 + tc) % 2
                            g = gbank[pb]
                            g.use(te)
                            if fs == 0 and tc == 0:
                                P.waits(te, aT_toks)
                                P.wait(te, w_up_tok)
                                gw1.use(te)
                            for dt in range(8):
                                te.emit(nc.tensor.matmul(
                                    fbank(pb),
                                    wv(par * 4096 + dt * 512 + fs * 128, 128),
                                    aTv(dt, tc * 512, (tc + 1) * 512),
                                    start=(dt == 0), stop=(dt == 7)))
                            g.done(te)
                            mm_tok = te.mark()
                            g.use(act)
                            P.wait(act, mm_tok)
                            act.emit(nc.scalar.activation(
                                bigB[:, par * 8192 + fs * 2048 + tc * 512:
                                     par * 8192 + fs * 2048 + (tc + 1) * 512],
                                fbank(pb), AF.Silu))
                            g.done(act)
                            hT_toks.append(act.mark())
                    gw1.done(te)
                    gaT.done(te)
                    last = idx == len(ex_list) - 1
                    for t in range(NT):
                        pair = 2 + (t % 2) * 2
                        g0, g1 = gbank[pair], gbank[pair + 1]
                        g0.use(te)
                        g1.use(te)
                        if t == 0:
                            P.waits(te, hT_toks)
                            P.wait(te, w_dn_tok)
                            gw2.use(te)
                        for half in range(2):
                            pbk = pair + half
                            for fs in range(4):
                                te.emit(nc.tensor.matmul(
                                    fbank(pbk),
                                    bigB[:, par * 8192 + fs * 2048 + t * 128:
                                         par * 8192 + fs * 2048 +
                                         (t + 1) * 128],
                                    wv(8192 + par * 4096 + fs * 1024 +
                                       half * 512, 512),
                                    start=(fs == 0), stop=(fs == 3)))
                        g0.done(te)
                        g1.done(te)
                        mm_tok = te.mark()
                        g0.use(dve)
                        g1.use(dve)
                        P.wait(dve, mm_tok)
                        P.wait(dve, xw_tok[t])
                        if eix == -1:
                            dve.emit(nc.vector.tensor_tensor(
                                xt(t), fpair(pair), xt(t), op=ALU.add))
                        else:
                            P.wait(dve, cw_toks[t])
                            kw = {}
                            if last:
                                kw = dict(accum_out=sm(SUMS + t))
                            dve.emit(nc.vector.scalar_tensor_tensor(
                                xt(t), fpair(pair), sm(CW + t * 64 + eix),
                                xt(t), op0=ALU.mult, op1=ALU.add, **kw))
                        g0.done(dve)
                        g1.done(dve)
                        xw_tok[t] = dve.mark()
                    gw2.done(te)
                sums = [xw_tok[t] for t in range(NT)]
                if n_experts == 0:
                    for t in range(NT):
                        P.wait(dve, xw_tok[t])
                        dve.emit(nc.vector.reduce_sum(sm(SUMS + t), xt(t),
                                                      axis=AX.X))
                    sums = [dve.mark()] * NT

            if not do_moe:
                for t in range(NT):
                    P.wait(dve, xw_tok[t])
                    dve.emit(nc.vector.reduce_sum(sm(SUMS + t), xt(t),
                                                  axis=AX.X))
                sums = [dve.mark()] * NT
            P.waits(dve, sums)
            dve.emit(nc.vector.tensor_scalar(
                sm(NEGMU, NT), sm(SUMS, NT), -1.0 / D, None, op0=ALU.mult))
            negmu_tok = dve.mark()
            P.wait(act, negmu_tok)
            for t in range(NT):
                engv = act if t % 2 == 0 else dve
                if engv is act:
                    stage_use_all(engv)
                    P.wait(engv, xw_tok[t])
                    engv.emit(nc.scalar.activation(
                        stage[:], xt(t), AF.Square, bias=sm(NEGMU + t),
                        scale=1.0, accum_out=sm(VARSS + t)))
                    gstage.done(engv)
                else:
                    gbank[0].use(engv)
                    gbank[1].use(engv)
                    P.wait(engv, xw_tok[t])
                    P.wait(engv, negmu_tok)
                    engv.emit(nc.vector.scalar_tensor_tensor(
                        fpair(0), xt(t), sm(NEGMU + t), xt(t),
                        op0=ALU.add, op1=ALU.mult,
                        accum_out=sm(VARSS + t)))
                    gbank[0].done(engv)
                    gbank[1].done(engv)
            vs_tok = act.mark()
            P.wait(dve, vs_tok)
            P.selfbar(dve)
            dve.emit(nc.vector.tensor_scalar(
                sm(STD, NT), sm(VARSS, NT), 1.0 / D, 1e-5,
                op0=ALU.mult, op1=ALU.add))
            pre_tok = dve.mark()
            P.wait(act, pre_tok)
            act.emit(nc.scalar.activation(sm(STD, NT), sm(STD, NT), AF.Sqrt))
            std_tok = act.mark()
            P.wait(dve, std_tok)
            dve.emit(nc.vector.reciprocal(sm(RECR, NT), sm(STD, NT)))
            P.selfbar(dve)
            dve.emit(nc.vector.tensor_tensor(
                sm(MUR, NT), sm(NEGMU, NT), sm(RECR, NT), op=ALU.mult))
            mur_tok = dve.mark()
            P.wait(act, mur_tok)
            for t in range(NT):
                stage_use_all(act)
                act.emit(nc.scalar.activation(
                    stage[:], xt(t), AF.Identity, bias=sm(MUR + t),
                    scale=sm(RECR + t)))
                nt_tok = act.mark()
                st_tok = P.dma(sp, "o", out[t * 128:(t + 1) * 128, :],
                               stage[:], deps=[nt_tok])
                gstage.done_tok("sp", st_tok)
        else:
            for t in range(NT):
                P.wait(sp, xw_tok[t])
                P.dma(sp, "o", out[t * 128:(t + 1) * 128, :], xt(t))

        for tn, shape, dt in taps:
            ap = tap_outs[tn]
            for eng in (act, dve, te):
                tk = eng.mark()
                if tk is not None:
                    P.wait(sp, tk)
            if tn == "aT":
                P.dma(sp, "o", ap, aT[:])
            elif tn == "oT":
                P.dma(sp, "o", ap, bigB[:])
            elif tn == "x":
                xvv = ap.rearrange("(a p) d -> p a d", p=128)
                P.dma(sp, "o", xvv, xv)
            elif tn == "qkvA":
                P.dma(sp, "o", ap, qkvA[:])
            elif tn == "smalls":
                P.dma(sp, "o", ap, smalls[:])

        for kind in ("w", "x", "o"):
            pool = P.pools[kind]
            for sem, uses in zip(pool["sems"], pool["uses"]):
                if uses:
                    sp.e.wait_ge(sem, 16 * uses)
    return nc


def _make_consts():
    bf = ml_dtypes.bfloat16
    c = np.zeros((128, 672), dtype=bf)
    c[:, 0:128] = np.eye(128, dtype=np.float32).astype(bf)
    m = np.zeros((128, 256), dtype=np.float32)
    m[:, 128:256] = np.triu(np.full((128, 128), -1e30, np.float32), k=1)
    c[:, 128:384] = m.astype(bf)
    c[:, 384:512] = np.triu(np.full((128, 128), -1e30, np.float32),
                            k=1).astype(bf)
    c[0, 512:640] = bf(1.0)
    c[:, 640] = bf(1.0)
    return c


def _prep_inputs(inputs):
    f32 = np.float32
    g = {k: np.asarray(v) for k, v in inputs.items()}
    com = {}
    com["Wq"] = np.ascontiguousarray(g["L0_attn_Wq"], f32)
    com["Wk"] = np.ascontiguousarray(g["L0_attn_Wk"], f32)
    com["Wv"] = np.ascontiguousarray(g["L0_attn_Wv"], f32)
    com["Wo"] = np.ascontiguousarray(g["L0_attn_Wo"], f32)
    com["bo"] = np.ascontiguousarray(g["L0_attn_bo"], f32)
    com["W1"] = np.ascontiguousarray(g["L0_attn_W1"], f32)
    com["W2"] = np.ascontiguousarray(g["L0_attn_W2"], f32)
    com["b2"] = np.ascontiguousarray(g["L0_attn_b2"], f32)
    com["g1T"] = np.ascontiguousarray(
        g["L0_attn_ln1_g"].reshape(DEPTH, 8, 128).transpose(0, 2, 1), f32)
    com["b1nT"] = np.ascontiguousarray(
        g["L0_attn_ln1_b"].reshape(DEPTH, 8, 128).transpose(0, 2, 1), f32)
    com["g2T"] = np.ascontiguousarray(
        g["L0_attn_ln2_g"].reshape(DEPTH, 8, 128).transpose(0, 2, 1), f32)
    com["b2nT"] = np.ascontiguousarray(
        g["L0_attn_ln2_b"].reshape(DEPTH, 8, 128).transpose(0, 2, 1), f32)
    com["b1T"] = np.ascontiguousarray(
        g["L0_attn_b1"].reshape(DEPTH, 32, 128).transpose(0, 2, 1), f32)
    com["ctrT"] = np.ascontiguousarray(g["L0_ffn_centroids"].T, f32)
    com["ebb"] = np.ascontiguousarray(
        np.broadcast_to(g["L0_ffn_expert_biases"], (128, NE)), f32)
    com["Wsu"] = np.ascontiguousarray(g["L0_ffn_Ws_up"], f32)
    com["Wsd"] = np.ascontiguousarray(g["L0_ffn_Ws_down"], f32)
    com["Wup"] = np.ascontiguousarray(g["L0_ffn_W_up"], f32)
    com["Wdn"] = np.ascontiguousarray(g["L0_ffn_W_down"], f32)
    com["cst"] = _make_consts()
    return g, com


def kernel(**inputs):
    g, com = _prep_inputs(inputs)
    x_full = np.ascontiguousarray(np.asarray(g["L0_attn_x"], np.float32))
    nc = build_program()
    in_maps = []
    for i in range(8):
        m = dict(com)
        m["x"] = x_full[i]
        in_maps.append(m)
    res = run_bass_kernel_spmd(nc, in_maps, core_ids=list(range(8)))
    outs = [r["out"] for r in res.results]
    return np.stack(outs, axis=0).astype(np.float32)
